# version 05
# baseline (speedup 1.0000x reference)
import numpy as np
import ml_dtypes
import concourse.bass as bass
import concourse.mybir as mybir
from concourse.bass_utils import run_bass_kernel_spmd
from concourse.alu_op_type import AluOpType
from contextlib import ExitStack

F32 = mybir.dt.float32
BF16 = mybir.dt.bfloat16
I32 = mybir.dt.int32
U32 = mybir.dt.uint32
AF = mybir.ActivationFunctionType
bf16 = ml_dtypes.bfloat16

D = 1024
H = 64
NB = D // 32
LN2 = float(np.log(2.0))
EXPB = -20.0
EPS = 1e-5
TINY = 1e-30
MAGIC_C1 = (22 << 23) | 0x400000
EXPMASK = 0x7F800000
ONEBITS = 127 << 23


class Prog:
    ENGINES = ("sync", "tensor", "vector", "scalar", "gpsimd")

    def __init__(self, nc):
        self.nc = nc
        self.ops = {e: [] for e in self.ENGINES}
        self.seq = {e: 0 for e in self.ENGINES}
        self.sems = {}
        self.dcnt = {}
        self.state = {}
        self.water = {}
        self._es = ExitStack()
        self.POOL_N = 12
        self.EPOOL_N = 8
        self.pool_rr2 = {}
        self.epool_rr = {e: 0 for e in self.ENGINES}

    def sem(self, name):
        if name not in self.sems:
            self.sems[name] = self._es.enter_context(self.nc.semaphore(name))
        return self.sems[name]

    def _wait(self, eng, marker):
        if marker is None:
            return None
        sname, val = marker
        key = (eng, sname)
        if self.water.get(key, 0) >= val:
            return None
        self.water[key] = val
        return (self.sems[sname], val)

    def op(self, eng, fn, reads=(), writes=(), dma=None, manual_sem=None):
        waits = []

        def add(m):
            w = self._wait(eng, m)
            if w:
                waits.append(w)

        for b in reads:
            st = self.state.get(b)
            if st and st["w"]:
                add(st["w"])
        for b in writes:
            st = self.state.get(b)
            if st:
                if st["w"]:
                    add(st["w"])
                for r in st["r"]:
                    add(r)
        if dma is not None or manual_sem is not None:
            pk = "sw" if eng == "gpsimd" else "hw"
            j = self.pool_rr2.get(pk, 0) % self.POOL_N
            self.pool_rr2[pk] = self.pool_rr2.get(pk, 0) + 1
            sname = f"dp_{pk}{j}"
            self.sem(sname)
            prev = self.dcnt.get(sname, 0)
            if prev:
                w = self._wait(eng, (sname, prev))
                if w:
                    waits.append(w)
            self.dcnt[sname] = prev + 16
            marker = (sname, prev + 16)
            if manual_sem is not None:
                realfn = fn
                semh = self.sems[sname]
                fn = (lambda realfn=realfn, semh=semh: (lambda e: realfn(e, semh)))()
                inc = None
            else:
                inc = (self.sems[sname], 16)
        else:
            j = self.epool_rr[eng] % self.EPOOL_N
            self.epool_rr[eng] += 1
            sname = f"e_{eng}{j}"
            self.sem(sname)
            prev = self.dcnt.get(sname, 0)
            if prev:
                w = self._wait(eng, (sname, prev))
                if w:
                    waits.append(w)
            self.dcnt[sname] = prev + 1
            marker = (sname, prev + 1)
            inc = (self.sems[sname], 1)
        for b in reads:
            self.state.setdefault(b, {"w": None, "r": []})["r"].append(marker)
        for b in writes:
            self.state[b] = {"w": marker, "r": []}
        self.ops[eng].append((waits, fn, inc))
        return marker

    def raw(self, eng, fn):
        self.ops[eng].append(([], fn, None))

    def emit(self):
        nc = self.nc
        with nc.Block() as block:
            for ename, deco in (("sync", block.sync), ("tensor", block.tensor),
                                ("vector", block.vector), ("scalar", block.scalar),
                                ("gpsimd", block.gpsimd)):
                ops = self.ops[ename]

                def make(ops_list):
                    def body(eng):
                        for waits, fn, inc in ops_list:
                            for sem, val in waits:
                                eng.wait_ge(sem, val)
                            if fn is None:
                                continue
                            inst = fn(eng)
                            if inc is not None:
                                if inst is None:
                                    eng.sem_inc(inc[0], inc[1])
                                else:
                                    inst.then_inc(inc[0], inc[1])
                    return body
                deco(make(ops))


KILL = {}


def build_nc(R=2048, KPG=4, debug_taps=False, stage=9):
    S = 4 * R
    NT = R // 128
    KT = S // 128
    NG = KT // KPG
    assert NG % 2 == 0
    CH = min(512, R)
    NCH = R // CH
    ECH = min(512, R)
    NEC = R // ECH
    CPE = ECH // CH
    COLH = min(2048, R)
    NCOLH = (2 * R) // COLH
    CC_N = COLH // 512 if COLH >= 512 else 1
    CW = min(512, COLH)

    nc = bass.Bass("TRN2", target_bir_lowering=False, debug=False,
                   detect_race_conditions=False)

    x_own = nc.dram_tensor("x_own", [R, D], F32, kind="ExternalInput")
    x_red = nc.dram_tensor("x_red", [R, D], F32, kind="ExternalInput")
    xT_half = nc.dram_tensor("xT_half", [D, 2 * R], F32, kind="ExternalInput")
    wq_in = nc.dram_tensor("wq_in", [D, H], F32, kind="ExternalInput")
    wk_in = nc.dram_tensor("wk_in", [D, H], F32, kind="ExternalInput")
    wv_in = nc.dram_tensor("wv_in", [D, H], F32, kind="ExternalInput")
    w_half = nc.dram_tensor("w_half", [D // 2, D], F32, kind="ExternalInput")
    dw_half = nc.dram_tensor("dw_half", [D // 2, NB], F32, kind="ExternalInput")
    bias_in = nc.dram_tensor("bias_in", [1, D], F32, kind="ExternalInput")
    dact_in = nc.dram_tensor("dact_in", [128, 1], F32, kind="ExternalInput")
    ident_bf = nc.dram_tensor("ident_bf", [128, 128], BF16, kind="ExternalInput")
    par_in = nc.dram_tensor("par_in", [1, 1], U32, kind="ExternalInput")
    y_out = nc.dram_tensor("y_out", [R, D], F32, kind="ExternalOutput")

    xq_dram = nc.dram_tensor("xq_dram", [R, D], BF16)
    mr_dram = nc.dram_tensor("mr_dram", [64, 128], BF16)
    delay_dram = nc.dram_tensor("delay_dram", [128, 8192], F32)
    delay_dram2 = nc.dram_tensor("delay_dram2", [128, 8192], F32)
    hello_flag = nc.dram_tensor("hello_flag", [2, 16], U32, addr_space="Shared")
    kv_sh = nc.dram_tensor("kv_sh", [2 * 128, 2 * R], BF16, addr_space="Shared")
    rr_sh = nc.dram_tensor("rr_sh", [2 * 128, 2 * NT], F32, addr_space="Shared")
    kv_flag = nc.dram_tensor("kv_flag", [2, 16], U32, addr_space="Shared")
    z_sh = nc.dram_tensor("z_sh", [2 * NG, 128, KPG], F32, addr_space="Shared")
    z_flag = nc.dram_tensor("z_flag", [2 * NG, 16], U32, addr_space="Shared")
    wq_sh = nc.dram_tensor("wq_sh", [D, D], BF16, addr_space="Shared")
    wq_loc = nc.dram_tensor("wq_loc", [D, D], BF16)
    wq_flag = nc.dram_tensor("wq_flag", [2, 16], U32, addr_space="Shared")

    sb = lambda n, s, d: nc.alloc_sbuf_tensor(n, s, d)

    def sb_at(name, shape, dtype, host):
        off = nc.lookup_mloc(host).addr
        return nc.alloc_sbuf_tensor_at(name, shape, dtype, offset=off)

    xt = [sb(f"xt{i}", [128, D], F32) for i in range(NT)]
    stat = sb("stat", [128, 12], F32)
    stat2 = sb("stat2", [128, 2], F32)
    m_own = sb("m_own", [128, NT], F32)
    r_own = sb("r_own", [128, NT], F32)
    m_red = sb("m_red", [128, NT], F32)
    r_red = sb("r_red", [128, NT], F32)
    negmr = sb("negmr", [128, NT], F32)
    negm = sb("negm", [128, NT], F32)
    var_t = sb("var_t", [128, NT], F32)
    mr_pack = sb("mr_pack", [128, 64], BF16)
    mrT = sb("mrT", [64, 128], BF16)
    m_row = sb("m_row", [1, 2 * R], BF16)
    rrow_tmp = sb("rrow_tmp", [1, 2 * R], BF16)
    r_rowq = sb("r_rowq", [64, 2 * R], BF16)
    r_cols = sb("r_cols", [128, 2 * KT], F32)
    idb = sb("idb", [128, 128], BF16)
    identf = sb("identf", [64, 64], F32)
    ones_row = sb("ones_row", [1, 128], BF16)
    ones_col = sb("ones_col", [128, 1], BF16)
    wkv_s = sb("wkv_s", [128, 8 * 128], BF16)
    wq_s = sb("wq_s", [128, 8 * 64], BF16)
    cw = sb("cw", [1, 192], BF16)
    xTs = [sb(f"xTs{i}", [128, COLH], BF16) for i in range(2)]
    kvT = sb("kvT", [128, 2 * R], BF16)
    qTo = sb("qTo", [128, R], BF16)
    qTr = sb("qTr", [128, R], BF16)
    kTd = sb("kTd", [128, S], BF16)
    vtok = sb("vtok", [128, KT * H], BF16)
    if 2 * S >= 5 * D + D:
        wt_in = kTd[:, 0:2 * D].bitcast(F32)
        wmg = kTd[:, 2 * D:4 * D].bitcast(F32)
        wq_bf = kTd[:, 4 * D:5 * D]
        WQ_ALIAS = ["kTd_m", "kTd_p"]
    else:
        wt_in = sb("wt_in", [128, D], F32)
        wmg = sb("wmg", [128, D], F32)
        wq_bf = sb("wq_bf", [128, D], BF16)
        WQ_ALIAS = []
    pbuf = [sb(f"pbuf{i}", [128, KPG * R], BF16) for i in range(3)]
    trash = sb("trash", [128, ECH], BF16)
    delay_t = trash[:, 0:ECH].bitcast(F32)
    zacc = sb("zacc", [128, KT * 8], F32)
    zmine = sb("zmine", [128, KT], F32)
    zpart = sb("zpart", [128, KPG], F32)
    zfull = sb("zfull", [128, KPG], F32)
    vzs = sb("vzs", [128, KPG], F32)
    vpp = [sb(f"vpp{i}", [128, KPG * H], BF16) for i in range(2)]
    stok = qTo[:, :].bitcast(F32)[:, 0:NT * H]
    if R >= 2 * D:
        xnt = qTr[:, :].bitcast(F32)[:, 0:D]
    else:
        xnt = sb("xnt", [128, D], F32)
    xqt = [sb(f"xqt{i}", [128, D], F32) for i in range(2)]
    xrt = xqt
    amax = sb("amax", [128, NB], F32)
    qs_rs = sb("qs_rs", [128, NB], F32)
    qs_sc = sb("qs_sc", [128, NB], F32)
    if COLH >= 2 * D:
        mg_t = xTs[1][:, :].bitcast(F32)
    else:
        mg_t = sb("mg_t", [128, D], F32)
    wamax = sb("wamax", [128, NB], F32)
    wrs = sb("wrs", [128, NB], F32)
    wsc = sb("wsc", [128, NB], F32)
    wfac = sb("wfac", [128, 4 * NB], F32)
    wrfac = sb("wrfac", [128, 4 * NB], F32)
    dwt = sb("dwt", [128, 4 * NB], F32)
    cact = sb("cact", [128, 4], F32)
    cvec = sb("cvec", [128, 2], F32)
    bias_bf = sb("bias_bf", [1, D], BF16)
    if COLH >= 2 * D:
        gt = xTs[0][:, :].bitcast(F32)
        ytt = xTs[1][:, :].bitcast(F32)
    else:
        gt = sb("gt", [128, D], F32)
        ytt = sb("ytt", [128, D], F32)
    flagt = sb("flagt", [1, 4], U32)
    expb_t = sb("expb_t", [128, 1], F32)
    onep = sb("onep", [128, 1], F32)
    lnb_t = sb("lnb_t", [128, 2], F32)
    onesu = sb("onesu", [1, 16], U32)
    part = sb("part", [1, 1], U32)

    sfeat = kvT[0:64, :].bitcast(F32)
    vstg = pbuf[0][0:64, 0:2 * R]

    if 2 * KPG * R >= 8 * R and KPG * R >= 8 * D:
        def xqT_strip(d):
            b = d // KPG
            o = (d % KPG) * R
            return pbuf[b][:, o:o + R]

        def wqT_strip(d):
            return pbuf[2][:, d * D:(d + 1) * D]
        XQT_KEYS = [("pbuf", 0), ("pbuf", 1)]
        WQT_KEYS = [("pbuf", 2)]
    else:
        xqTs_t = sb("xqTs", [128, 8 * R], BF16)
        wqTs_t = sb("wqTs", [128, 8 * D], BF16)

        def xqT_strip(d):
            return xqTs_t[:, d * R:(d + 1) * R]

        def wqT_strip(d):
            return wqTs_t[:, d * D:(d + 1) * D]
        XQT_KEYS = ["xqTs_t"]
        WQT_KEYS = ["wqTs_t"]

    psA = nc.alloc_psum_tensor("psA", [128, 1024], F32)
    psB = nc.alloc_psum_tensor("psB", [128, 1024], F32)
    out_ps = nc.alloc_psum_tensor("out_ps", [128, R], F32)
    PSA = lambda h: ("psA", h)
    PSB = lambda h: ("psB", h)

    P = Prog(nc)
    preg = nc.alloc_register(mybir.EngineType.Pool, "preg")
    spreg = nc.alloc_register(mybir.EngineType.Pool, "spreg")
    RH = bass.RegisterHandles
    def bailout():
        marks = []
        for u in range(NT):
            m = P.op("sync", lambda e, u=u: e.dma_start(out=y_out[u * 128:(u + 1) * 128, :],
                                                        in_=xt[u][:]),
                     reads=[("xt", u)], dma="yo")
            marks.append(m)

        def fw(e):
            for sname, val in marks:
                e.wait_ge(P.sems[sname], val)
            return None
        P.raw("sync", fw)
        P.emit()


    def cond2(e, sem, fn0, fn1):
        with nc.If_cmp(RH((preg,)), 0, "IS_EQ"):
            fn0(e).then_inc(sem, 16)
        nc.end_ifs()
        with nc.If_cmp(RH((preg,)), 0, "IS_NE"):
            fn1(e).then_inc(sem, 16)
        nc.end_ifs()
        return None

    spreg2 = nc.alloc_register(mybir.EngineType.Pool, "spreg2")

    def spin(tag, flag_row0, flag_row1, K, ND=12, DCH=8):
        bar = P.sem("bar_" + tag)
        pol = P.sem("pol")
        dly = P.sem("dly")
        p0 = P.dcnt.get("pol", 0)
        NPOLL = K + ND

        total = 2 * NPOLL

        def f(e):
            cnt = p0
            for k in range(NPOLL):
                if k >= K:
                    with nc.If_cmp(bass.RegisterHandles((spreg,)), 2, "IS_LT"):
                        for _ in range(DCH):
                            e.dma_start(out=delay_dram2[:, :],
                                        in_=delay_dram[:, :]).then_inc(dly, 16)
                    nc.end_ifs()
                e.dma_start(out=flagt[0:1, 0:1], in_=flag_row0).then_inc(pol, 16)
                e.dma_start(out=flagt[0:1, 1:2], in_=flag_row1).then_inc(pol, 16)
                cnt += 32
                e.wait_ge(pol, cnt)
                e.reg_load(spreg, flagt[0:1, 0:1])
                e.reg_load(spreg2, flagt[0:1, 1:2])
                e.reg_alu(spreg, spreg, spreg2, AluOpType.add)
                with nc.If_cmp(bass.RegisterHandles((spreg,)), 2, "IS_GE"):
                    e.sem_inc(bar, 1)
                nc.end_ifs()
            e.wait_ge(bar, 1)
            return None
        P.dcnt["pol"] = p0 + 16 * total
        P.op("gpsimd", f, reads=["part_reg", "flagt", "delay_dram"],
             writes=[("bar", tag), "flagt"])
        return ("bar_" + tag, 1)

    P.op("sync", lambda e: e.dma_start(out=idb[:], in_=ident_bf[:, :]), writes=["idb"], dma="cst")
    P.op("gpsimd", lambda e: e.dma_start(out=bias_bf[:], in_=bias_in[:, :]), writes=["bias_bf"], dma="cst2")
    P.op("sync", lambda e: e.dma_start(out=cact[:, 0:1], in_=dact_in[:, :]), writes=["cact0"], dma="cst")
    P.op("sync", lambda e: e.dma_start(out=part[:], in_=par_in[:, :]), writes=["part"], dma="cst")
    if stage <= -2:
        bailout()
        return nc

    P.op("gpsimd", lambda e: e.memset(onesu[:], 1), writes=["onesu"])
    P.op("gpsimd", lambda e: e.memset(ones_row[:], 1.0), writes=["ones_row"])
    P.op("gpsimd", lambda e: e.memset(ones_col[:], 1.0), writes=["ones_col"])
    P.op("gpsimd", lambda e: e.memset(expb_t[:], EXPB), writes=["expb"])
    P.op("gpsimd", lambda e: e.memset(onep[:], 1.0), writes=["onep"])
    if KILL.get("delayinit", False) is False:
        P.op("gpsimd", lambda e: e.memset(trash[:].bitcast(F32), 0.0), writes=["trash"])
        DCH_W = ECH // 2
        for ci in range(8192 // DCH_W):
            P.op("gpsimd", lambda e, ci=ci: e.dma_start(
                out=delay_dram[:, ci * DCH_W:(ci + 1) * DCH_W], in_=delay_t),
                reads=["trash"], writes=["delay_dram"], dma="gc")
    P.op("gpsimd", lambda e: e.memset(lnb_t[:, 0:1], -2.0 * LN2), writes=["lnb0"])
    P.op("gpsimd", lambda e: e.memset(lnb_t[:, 1:2], 1.0 * LN2), writes=["lnb1"])

    def loadreg(e):
        e.reg_load(preg, part[0:1, 0:1])
        return None
    P.op("gpsimd", loadreg, reads=["part"], writes=["part_reg"])

    P.op("gpsimd", lambda e, sem: cond2(
        e, sem, lambda e2: e2.dma_start(out=hello_flag[0:1, :], in_=onesu[:]),
        lambda e2: e2.dma_start(out=hello_flag[1:2, :], in_=onesu[:])),
        reads=["onesu", "part_reg"], writes=["hello_w"], manual_sem=True)
    spin("hello", hello_flag[0:1, 0:1], hello_flag[1:2, 0:1], K=4, ND=40, DCH=16)

    P.op("gpsimd", lambda e: e.dma_start(
        out=wkv_s[:].rearrange("p (t m) -> p t m", t=8)[:, :, 0:64],
        in_=wk_in[:, :].rearrange("(t p) h -> p t h", p=128)), writes=["wkv_k"], dma="gc")
    P.op("gpsimd", lambda e: e.dma_start(
        out=wkv_s[:].rearrange("p (t m) -> p t m", t=8)[:, :, 64:128],
        in_=wv_in[:, :].rearrange("(t p) h -> p t h", p=128)), writes=["wkv_v"], dma="gc")
    P.op("gpsimd", lambda e: e.dma_start(
        out=wq_s[:].rearrange("p (t h) -> p t h", t=8),
        in_=wq_in[:, :].rearrange("(t p) h -> p t h", p=128)), writes=["wq_s"], dma="gc")
    P.op("gpsimd", lambda e: e.dma_start(
        out=dwt[:].rearrange("p (t b) -> p t b", t=4),
        in_=dw_half[:, :].rearrange("(t p) b -> p t b", p=128)), writes=["dwt"], dma="gc")

    if stage <= -1:
        bailout()
        return nc

    P.op("scalar", lambda e: e.activation(cact[:, 2:3], cact[:, 0:1], AF.Exp, scale=8.0),
         reads=["cact0"], writes=["cact2"])
    P.op("vector", lambda e: e.tensor_scalar(cact[:, 2:3], cact[:, 2:3], 1.0, None,
                                             AluOpType.add),
         reads=["cact2"], writes=["cact2"])
    P.op("vector", lambda e: e.reciprocal(cact[:, 3:4], cact[:, 2:3]),
         reads=["cact2"], writes=["cact3"])
    P.op("vector", lambda e: e.tensor_scalar(cact[:, 1:2], cact[:, 3:4], -2.0, 1.0,
                                             AluOpType.mult, AluOpType.add),
         reads=["cact3"], writes=["cact1"])
    P.op("scalar", lambda e: e.activation(cvec[:, 0:1], cact[:, 1:2], AF.Exp,
                                          scale=0.5 * LN2, bias=lnb_t[:, 0:1]),
         reads=["cact1", "lnb0"], writes=["cvec"])
    P.op("scalar", lambda e: e.activation(cvec[:, 1:2], cact[:, 1:2], AF.Exp,
                                          scale=-0.5 * LN2, bias=lnb_t[:, 1:2]),
         reads=["cact1", "lnb1", "cvec"], writes=["cvec"])

    def cw_mms(e):
        last = None
        for t in range(8):
            last = e.matmul(psB[0:1, 512:640], ones_col[:, :], wkv_s[:, t * 128:(t + 1) * 128],
                            start=(t == 0), stop=(t == 7))
        for t in range(8):
            last = e.matmul(psB[0:1, 640:704], ones_col[:, :], wq_s[:, t * 64:(t + 1) * 64],
                            start=(t == 0), stop=(t == 7))
        return last
    P.op("tensor", cw_mms, reads=["wkv_k", "wkv_v", "wq_s", "ones_col"], writes=[PSB(1)])
    P.op("vector", lambda e: e.tensor_copy(cw[0:1, :], psB[0:1, 512:704]),
         reads=[PSB(1)], writes=["cw"])

    for i in range(NT):
        P.op("sync", lambda e, i=i: e.dma_start(out=xt[i][:], in_=x_own[i * 128:(i + 1) * 128, :]),
             writes=[("xt", i)], dma="xo")
    for i in range(NT):
        P.op("vector", lambda e, i=i: e.bn_stats(stat[:, 0:6], xt[i][:, 0:512]),
             reads=[("xt", i)], writes=["stat6a"])
        P.op("vector", lambda e, i=i: e.bn_stats(stat[:, 6:12], xt[i][:, 512:1024]),
             reads=[("xt", i)], writes=["stat6b"])
        P.op("vector", lambda e: e.bn_aggr(stat2[:, 0:2], stat[:, 0:12]),
             reads=["stat6a", "stat6b"], writes=["stat"])
        P.op("vector", lambda e, i=i: e.tensor_copy(m_own[:, i:i + 1], stat2[:, 0:1]),
             reads=["stat"], writes=["m_own"])
        P.op("vector", lambda e, i=i: e.tensor_scalar(var_t[:, i:i + 1], stat2[:, 1:2], EPS, None,
                                                      AluOpType.add),
             reads=["stat"], writes=["var"])
    P.op("scalar", lambda e: e.activation(var_t[:], var_t[:], AF.Ln), reads=["var"], writes=["var"])
    P.op("scalar", lambda e: e.activation(r_own[:], var_t[:], AF.Exp, scale=-0.5),
         reads=["var"], writes=["r_own"])
    for i in range(NT):
        P.op("scalar", lambda e, i=i: e.dma_start(out=xrt[i % 2][:], in_=x_red[i * 128:(i + 1) * 128, :]),
             writes=[("xqt", i % 2)], dma="xr")
        P.op("vector", lambda e, i=i: e.bn_stats(stat[:, 0:6], xrt[i % 2][:, 0:512]),
             reads=[("xqt", i % 2)], writes=["stat6a"])
        P.op("vector", lambda e, i=i: e.bn_stats(stat[:, 6:12], xrt[i % 2][:, 512:1024]),
             reads=[("xqt", i % 2)], writes=["stat6b"])
        P.op("vector", lambda e: e.bn_aggr(stat2[:, 0:2], stat[:, 0:12]),
             reads=["stat6a", "stat6b"], writes=["stat"])
        P.op("vector", lambda e, i=i: e.tensor_copy(m_red[:, i:i + 1], stat2[:, 0:1]),
             reads=["stat"], writes=["m_red"])
        P.op("vector", lambda e, i=i: e.tensor_scalar(var_t[:, i:i + 1], stat2[:, 1:2], EPS, None,
                                                      AluOpType.add),
             reads=["stat"], writes=["varr"])
    P.op("scalar", lambda e: e.activation(var_t[:], var_t[:], AF.Ln), reads=["varr"], writes=["varr"])
    P.op("scalar", lambda e: e.activation(r_red[:], var_t[:], AF.Exp, scale=-0.5),
         reads=["varr"], writes=["r_red"])
    P.op("vector", lambda e: e.tensor_mul(negmr[:], m_own[:], r_own[:]),
         reads=["m_own", "r_own"], writes=["negmr"])
    P.op("vector", lambda e: e.tensor_scalar(negmr[:], negmr[:], -1.0, None, AluOpType.mult),
         reads=["negmr"], writes=["negmr"])
    P.op("vector", lambda e: e.tensor_scalar(negm[:], m_own[:], -1.0, None, AluOpType.mult),
         reads=["m_own"], writes=["negm"])

    P.op("vector", lambda e: e.memset(mr_pack[:], 0.0), writes=["mrp"])
    P.op("vector", lambda e: e.tensor_scalar(mr_pack[:, 0:NT], m_own[:], -1.0, None, AluOpType.mult),
         reads=["m_own", "mrp"], writes=["mrp"])
    P.op("vector", lambda e: e.tensor_scalar(mr_pack[:, 16:16 + NT], m_red[:], -1.0, None,
                                             AluOpType.mult),
         reads=["m_red", "mrp"], writes=["mrp"])
    P.op("vector", lambda e: e.tensor_copy(mr_pack[:, 32:32 + NT], r_own[:]),
         reads=["r_own", "mrp"], writes=["mrp"])
    P.op("vector", lambda e: e.tensor_copy(mr_pack[:, 48:48 + NT], r_red[:]),
         reads=["r_red", "mrp"], writes=["mrp"])
    P.op("tensor", lambda e: e.transpose(psB[0:64, 0:64].bitcast(BF16), mr_pack[:, 0:64], idb[:, :]),
         reads=["mrp", "idb"], writes=[PSB(0)])
    P.op("vector", lambda e: e.tensor_copy(mrT[:, :], psB[0:64, 0:64].bitcast(BF16)),
         reads=[PSB(0)], writes=["mrT"])
    P.op("gpsimd", lambda e: e.dma_start(out=mr_dram[:, :], in_=mrT[:, :]),
         reads=["mrT"], writes=["mr_dram"], dma="mr")
    P.op("sync", lambda e: e.dma_start(
        out=m_row[:, :].rearrange("o (h t p) -> o h t p", h=2, t=NT),
        in_=mr_dram[0:32, :].rearrange("(o h t) p -> o h t p", o=1, h=2)[:, :, 0:NT]),
        reads=["mr_dram"], writes=["m_row"], dma="mrr")
    P.op("sync", lambda e: e.dma_start(
        out=rrow_tmp[:, :].rearrange("o (h t p) -> o h t p", h=2, t=NT),
        in_=mr_dram[32:64, :].rearrange("(o h t) p -> o h t p", o=1, h=2)[:, :, 0:NT]),
        reads=["mr_dram"], writes=["rrow_tmp"], dma="mrr")
    for ci in range(2 * R // 512):
        half = ci % 2
        P.op("tensor", lambda e, ci=ci, half=half: e.matmul(
            psA[0:64, half * 512:half * 512 + 512], ones_row[0:1, 0:64],
            rrow_tmp[0:1, ci * 512:(ci + 1) * 512], start=True, stop=True),
            reads=["rrow_tmp", "ones_row"], writes=[PSA(half)])
        P.op("vector", lambda e, ci=ci, half=half: e.tensor_copy(
            r_rowq[:, ci * 512:(ci + 1) * 512], psA[0:64, half * 512:half * 512 + 512]),
            reads=[PSA(half)], writes=["r_rowq"])
    P.op("vector", lambda e: e.tensor_copy(r_cols[:, 0:NT], r_own[:]),
         reads=["r_own"], writes=["r_cols_m"])
    P.op("vector", lambda e: e.tensor_copy(r_cols[:, NT:2 * NT], r_red[:]),
         reads=["r_red", "r_cols_m"], writes=["r_cols_m"])

    if stage <= 0:
        bailout()
        return nc

    for ch in range(NCOLH):
        buf = ch % 2
        c0 = ch * COLH
        for d in range(8):
            P.op("gpsimd", lambda e, d=d, c0=c0, buf=buf: e.dma_start(
                out=xTs[buf][:], in_=xT_half[d * 128:(d + 1) * 128, c0:c0 + COLH]),
                writes=[("xTs", buf)], dma="xT")

            def mmkv(e, d=d, buf=buf):
                last = None
                for cc in range(CC_N):
                    last = e.matmul(out_ps[:, cc * 512:cc * 512 + CW],
                                    wkv_s[:, d * 128:(d + 1) * 128],
                                    xTs[buf][:, cc * 512:cc * 512 + CW],
                                    start=(d == 0), stop=False)
                for cc in range(CC_N):
                    tgt, off = (psA, (cc % 2) * 512) if cc < 2 else (psB, (cc % 2) * 512)
                    last = e.matmul(tgt[0:64, off:off + CW],
                                    wq_s[:, d * 64:(d + 1) * 64],
                                    xTs[buf][:, cc * 512:cc * 512 + CW],
                                    start=(d == 0), stop=False)
                return last
            P.op("tensor", mmkv, reads=[("xTs", buf), "wkv_k", "wkv_v", "wq_s"],
                 writes=[("outps",), PSA(0), PSA(1), PSB(0), PSB(1)])

        def fix(e, c0=c0):
            last = None
            for cc in range(CC_N):
                last = e.matmul(out_ps[:, cc * 512:cc * 512 + CW], cw[0:1, 0:128],
                                m_row[0:1, c0 + cc * 512:c0 + cc * 512 + CW],
                                start=False, stop=True)
                tgt, off = (psA, (cc % 2) * 512) if cc < 2 else (psB, (cc % 2) * 512)
                last = e.matmul(tgt[0:64, off:off + CW], cw[0:1, 128:192],
                                m_row[0:1, c0 + cc * 512:c0 + cc * 512 + CW],
                                start=False, stop=True)
            return last
        P.op("tensor", fix, reads=["cw", "m_row", ("outps",), PSA(0), PSA(1), PSB(0), PSB(1)],
             writes=[("outps",), PSA(0), PSA(1), PSB(0), PSB(1)])
        for cc in range((COLH + 511) // 512):
            wdt = min(512, COLH - cc * 512)
            P.op("vector", lambda e, c0=c0, cc=cc, wdt=wdt: e.tensor_copy(
                kvT[:, c0 + cc * 512:c0 + cc * 512 + wdt], out_ps[:, cc * 512:cc * 512 + wdt]),
                reads=[("outps",)], writes=["kvT"])

        for cc in range(CC_N):
            tgt, off = (psA, (cc % 2) * 512) if cc < 2 else (psB, (cc % 2) * 512)
            gc0 = c0 + cc * 512
            dst = qTo if gc0 < R else qTr
            do = gc0 % R
            pk = PSA(cc % 2) if cc < 2 else PSB(cc % 2)
            P.op("vector", lambda e, tgt=tgt, off=off, dst=dst, do=do, gc0=gc0:
                 e.scalar_tensor_tensor(dst[0:64, do:do + CW], tgt[0:64, off:off + CW],
                                        1.0, r_rowq[0:64, gc0:gc0 + CW],
                                        AluOpType.mult, AluOpType.mult),
                 reads=[pk, "r_rowq"], writes=["qT0"])
    P.op("vector", lambda e: e.tensor_copy(qTo[64:128, :], qTo[0:64, :]),
         reads=["qT0"], writes=["qTo"])
    P.op("vector", lambda e: e.tensor_copy(qTr[64:128, :], qTr[0:64, :]),
         reads=["qT0", "qTo"], writes=["qTr"])

    if stage <= -1:
        pass
    if stage == 5:
        bailout()
        return nc

    P.op("gpsimd", lambda e, sem: cond2(
        e, sem, lambda e2: e2.dma_start(out=kv_sh[0:128, :], in_=kvT[:]),
        lambda e2: e2.dma_start(out=kv_sh[128:256, :], in_=kvT[:])),
        reads=["kvT", "part_reg"], writes=["kv_sh_w"], manual_sem=True)
    P.op("gpsimd", lambda e, sem: cond2(
        e, sem, lambda e2: e2.dma_start(out=rr_sh[0:128, :], in_=r_cols[:, 0:2 * NT]),
        lambda e2: e2.dma_start(out=rr_sh[128:256, :], in_=r_cols[:, 0:2 * NT])),
        reads=["r_cols_m", "part_reg"], writes=["rr_sh_w"], manual_sem=True)
    P.op("gpsimd", lambda e, sem: cond2(
        e, sem, lambda e2: e2.dma_start(out=kv_flag[0:1, :], in_=onesu[:]),
        lambda e2: e2.dma_start(out=kv_flag[1:2, :], in_=onesu[:])),
        reads=["onesu", "part_reg", "kv_sh_w", "rr_sh_w"], writes=["kv_flag_w"],
        manual_sem=True)
    kv_bar = spin("kv", kv_flag[0:1, 0:1], kv_flag[1:2, 0:1], K=60)
    P.op("gpsimd", lambda e, sem: cond2(
        e, sem, lambda e2: e2.dma_start(out=kTd[0:64, 2 * R:4 * R], in_=kv_sh[128:192, :]),
        lambda e2: e2.dma_start(out=kTd[0:64, 2 * R:4 * R], in_=kv_sh[0:64, :])),
        reads=[("bar", "kv"), "part_reg"], writes=["kTd_p"], manual_sem=True)
    P.op("gpsimd", lambda e, sem: cond2(
        e, sem, lambda e2: e2.dma_start(out=kTd[64:128, 2 * R:4 * R], in_=kv_sh[128:192, :]),
        lambda e2: e2.dma_start(out=kTd[64:128, 2 * R:4 * R], in_=kv_sh[0:64, :])),
        reads=[("bar", "kv"), "part_reg", "kTd_p"], writes=["kTd_p"], manual_sem=True)
    P.op("gpsimd", lambda e, sem: cond2(
        e, sem, lambda e2: e2.dma_start(out=vstg, in_=kv_sh[192:256, :]),
        lambda e2: e2.dma_start(out=vstg, in_=kv_sh[64:128, :])),
        reads=[("bar", "kv"), "part_reg"], writes=[("pbuf", 0)], manual_sem=True)
    P.op("gpsimd", lambda e, sem: cond2(
        e, sem, lambda e2: e2.dma_start(out=r_cols[:, 2 * NT:4 * NT], in_=rr_sh[128:256, :]),
        lambda e2: e2.dma_start(out=r_cols[:, 2 * NT:4 * NT], in_=rr_sh[0:128, :])),
        reads=[("bar", "kv"), "part_reg", "r_cols_m"], writes=["r_cols"], manual_sem=True)
    P.op("vector", lambda e: e.tensor_copy(kTd[0:64, 0:2 * R], kvT[0:64, :]),
         reads=["kvT"], writes=["kTd_m"])
    P.op("vector", lambda e: e.tensor_copy(kTd[64:128, 0:2 * R], kvT[0:64, :]),
         reads=["kvT", "kTd_m"], writes=["kTd_m"])
    for t in range(KT):
        if t < KT // 2:
            src = kvT[64:128, t * 128:(t + 1) * 128]
            rk = ["kvT"]
            ident = idb[64:128, 64:128]
        else:
            src = vstg[:, (t - KT // 2) * 128:(t - KT // 2 + 1) * 128]
            rk = [("pbuf", 0)]
            ident = idb[0:64, 0:64]
        half = t % 2
        P.op("tensor", lambda e, src=src, half=half, ident=ident: e.transpose(
            psA[:, half * 512:half * 512 + 32].bitcast(BF16), src, ident),
            reads=rk + ["idb"], writes=[PSA(half)])
        P.op("vector", lambda e, t=t, half=half: e.tensor_copy(
            vtok[:, t * H:(t + 1) * H], psA[:, half * 512:half * 512 + 32].bitcast(BF16)),
            reads=[PSA(half)], writes=["vtok"])

    if stage <= 1:
        bailout()
        return nc

    P.op("vector", lambda e: e.memset(zacc[:], 0.0), writes=[("zacc", k) for k in range(KT * 8)])
    gorder = []
    for i in range(NG // 2):
        gorder += [i, i + NG // 2]

    def strip(tile, red, pslot, pb, pcol):
        ps_ = psA if pslot == 0 else psB
        pkey = PSA if pslot == 0 else PSB
        rg = 0 if pslot == 0 else 64
        q = qTr if red else qTo
        qk = "qTr" if red else "qTo"
        kslice = kTd[rg:rg + 64, tile * 128:(tile + 1) * 128]
        for ec in range(NEC):
            half = ec % 2
            def mms(e, ec=ec, kslice=kslice, ps_=ps_, rg=rg, q=q, half=half):
                last = e.ldweights(kslice, tile_position=(rg, 0))
                c = ec * ECH
                last = e.matmul(ps_[:, half * 512:half * 512 + ECH], kslice,
                                q[rg:rg + 64, c:c + ECH],
                                start=True, stop=True, tile_position=(rg, 0))
                return last
            P.op("tensor", mms, reads=["kTd_m", "kTd_p", qk], writes=[pkey(half)])
            acc_i = tile * 8 + (4 if red else 0) + ec
            if red:
                outap = trash[:, 0:ECH]
                wk = ["trash"]
            else:
                outap = pbuf[pb][:, pcol * R + ec * ECH: pcol * R + (ec + 1) * ECH]
                wk = [("pbuf", pb)]
            P.op("scalar", lambda e, outap=outap, ps_=ps_, acc_i=acc_i, tile=tile, half=half:
                 e.activation(outap, ps_[:, half * 512:half * 512 + ECH], AF.Exp,
                              bias=expb_t[:], scale=r_cols[:, tile:tile + 1],
                              accum_out=zacc[:, acc_i:acc_i + 1]),
                 reads=[pkey(half), "r_cols", "expb"],
                 writes=wk + [("zacc", acc_i)])

    def emit_group_strips(si, g):
        pb = si % 3
        for j in range(KPG):
            strip(g * KPG + j, False, j % 2, pb, j)
        for j in range(KPG):
            strip(g * KPG + j, True, j % 2, pb, j)

        def zsum(e, g=g):
            return e.reduce_sum(zmine[:, g * KPG:(g + 1) * KPG],
                                zacc[:, g * KPG * 8:(g + 1) * KPG * 8]
                                .rearrange("p (t f) -> p t f", f=8),
                                axis=mybir.AxisListType.X)
        P.op("vector", zsum,
             reads=[("zacc", g * KPG * 8 + k) for k in range(KPG * 8)], writes=[("zmine", g)])
        P.op("gpsimd", lambda e, sem, g=g: cond2(
            e, sem, lambda e2: e2.dma_start(out=z_sh[g:g + 1, :, :].rearrange("o p k -> p (o k)"),
                                       in_=zmine[:, g * KPG:(g + 1) * KPG]),
            lambda e2: e2.dma_start(out=z_sh[NG + g:NG + g + 1, :, :].rearrange("o p k -> p (o k)"),
                                    in_=zmine[:, g * KPG:(g + 1) * KPG])),
            reads=[("zmine", g), "part_reg"], writes=[("z_sh", g)], manual_sem=True)
        P.op("gpsimd", lambda e, sem, g=g: cond2(
            e, sem, lambda e2: e2.dma_start(out=z_flag[g:g + 1, :], in_=onesu[:]),
            lambda e2: e2.dma_start(out=z_flag[NG + g:NG + g + 1, :], in_=onesu[:])),
            reads=["onesu", "part_reg", ("z_sh", g)], writes=[("z_flag", g)], manual_sem=True)

    def emit_group_pv(si, g, is_first, is_last):
        pb = si % 3
        gp = (g + NG // 2) % NG
        spin(f"z{g}", z_flag[gp:gp + 1, 0:1], z_flag[NG + gp:NG + gp + 1, 0:1], K=20)
        P.op("gpsimd", lambda e, sem, gp=gp: cond2(
            e, sem, lambda e2: e2.dma_start(out=zpart[:, :],
                                       in_=z_sh[NG + gp:NG + gp + 1, :, :].rearrange("o p k -> p (o k)")),
            lambda e2: e2.dma_start(out=zpart[:, :],
                                    in_=z_sh[gp:gp + 1, :, :].rearrange("o p k -> p (o k)"))), reads=[("bar", f"z{g}"), "part_reg"], writes=["zpart"], manual_sem=True)
        P.op("vector", lambda e, g=g: e.tensor_add(zfull[:, :], zmine[:, g * KPG:(g + 1) * KPG],
                                                   zpart[:, :]),
             reads=[("zmine", g), "zpart"], writes=["zfull"])
        P.op("vector", lambda e: e.reciprocal(zfull[:], zfull[:]), reads=["zfull"], writes=["zfull"])
        P.op("vector", lambda e, g=g: e.tensor_mul(vzs[:, :], zfull[:, :],
                                                   r_cols[:, g * KPG:(g + 1) * KPG]),
             reads=["zfull", "r_cols"], writes=["vzs"])
        vslot = si % 2

        for j in range(KPG):
            t = g * KPG + j
            P.op("vector", lambda e, j=j, t=t, vslot=vslot:
                 e.tensor_scalar(vpp[vslot][:, j * H:(j + 1) * H], vtok[:, t * H:(t + 1) * H],
                                 vzs[:, j:j + 1], None, AluOpType.mult),
                 reads=["vtok", "vzs"], writes=[("vpp", vslot)])

        def pv(e, pb=pb, vslot=vslot, is_first=is_first, is_last=is_last):
            last = None
            for j in range(0, KPG, 2):
                e.ldweights(vpp[vslot][:, j * H:(j + 1) * H], tile_position=(0, 0))
                e.ldweights(vpp[vslot][:, (j + 1) * H:(j + 2) * H], tile_position=(0, 64))
                for cc in range(NCH):
                    last = e.matmul(out_ps[0:64, cc * CH:(cc + 1) * CH],
                                    vpp[vslot][:, j * H:(j + 1) * H],
                                    pbuf[pb][:, j * R + cc * CH:j * R + (cc + 1) * CH],
                                    start=(is_first and j == 0),
                                    stop=(is_last and j == KPG - 2),
                                    tile_position=(0, 0))
                    last = e.matmul(out_ps[64:128, cc * CH:(cc + 1) * CH],
                                    vpp[vslot][:, (j + 1) * H:(j + 2) * H],
                                    pbuf[pb][:, (j + 1) * R + cc * CH:(j + 1) * R + (cc + 1) * CH],
                                    start=(is_first and j == 0),
                                    stop=(is_last and j == KPG - 2),
                                    tile_position=(0, 64), skip_group_check=True)
            return last
        P.op("tensor", pv, reads=[("vpp", vslot), ("pbuf", pb)],
             writes=[("outps",), ("pbuf_pv", pb)])

    for si, g in enumerate(gorder):
        emit_group_strips(si, g)
        if si >= 1:
            emit_group_pv(si - 1, gorder[si - 1], si - 1 == 0, si - 1 == len(gorder) - 1)
    emit_group_pv(len(gorder) - 1, gorder[-1], len(gorder) - 1 == 0, True)

    for cc in range(R // 512 if R >= 512 else 1):
        wdt = min(512, R)
        P.op("vector", lambda e, cc=cc, wdt=wdt: e.tensor_copy(
            sfeat[:, cc * 512:cc * 512 + wdt], out_ps[0:64, cc * 512:cc * 512 + wdt]),
            reads=[("outps",)], writes=["sfeat", "kvT"])
        P.op("vector", lambda e, cc=cc, wdt=wdt: e.scalar_tensor_tensor(
            sfeat[:, cc * 512:cc * 512 + wdt], out_ps[64:128, cc * 512:cc * 512 + wdt], 1.0,
            sfeat[:, cc * 512:cc * 512 + wdt], AluOpType.mult, AluOpType.add),
            reads=[("outps",), "sfeat"], writes=["sfeat"])
    P.op("vector", lambda e: e.tensor_copy(identf[:, :], idb[0:64, 0:64]),
         reads=["idb"], writes=["identf"])
    for u in range(NT):
        half = u % 2
        P.op("tensor", lambda e, u=u, half=half: e.transpose(
            psA[:, half * 512:half * 512 + 64], sfeat[:, u * 128:(u + 1) * 128], identf[:, :]),
            reads=["sfeat", "identf"], writes=[PSA(half)])
        P.op("vector", lambda e, u=u, half=half: e.tensor_copy(
            stok[:, u * H:(u + 1) * H], psA[:, half * 512:half * 512 + 64]),
            reads=[PSA(half)], writes=["stok", "qTo"])

    if stage <= 2:
        bailout()
        return nc

    def quant_chain(eng, src_f, dst_f, am_t, rs_t, sc_t, xs_f, mg_f, n, pref, extra_reads,
                    fac_mul=None, fac_rs=None, dst_keys=()):
        nb = n // 32
        P.op("vector", lambda e: e.reduce_max(am_t[:, 0:nb],
                                              src_f().rearrange("p (b e) -> p b e", e=32),
                                              axis=mybir.AxisListType.X,
                                              apply_absolute_value=True),
             reads=extra_reads, writes=[pref + "_amax"])
        P.op(eng, lambda e: e.tensor_scalar(am_t[:, 0:nb], am_t[:, 0:nb], TINY, None,
                                            AluOpType.max),
             reads=[pref + "_amax"], writes=[pref + "_amax"])
        P.op(eng, lambda e: e.tensor_scalar(sc_t[:, 0:nb].bitcast(I32),
                                            am_t[:, 0:nb].bitcast(I32), EXPMASK, None,
                                            AluOpType.bitwise_and),
             reads=[pref + "_amax"], writes=[pref + "_msk"])
        P.op(eng, lambda e: e.tensor_scalar(rs_t[:, 0:nb].bitcast(I32),
                                            sc_t[:, 0:nb].bitcast(I32), EXPMASK, None,
                                            AluOpType.bitwise_xor),
             reads=[pref + "_msk"], writes=[pref + "_rsp0"])
        if fac_mul is not None:
            P.op(eng, lambda e: e.tensor_mul(sc_t[:, 0:nb], sc_t[:, 0:nb], fac_mul()),
                 reads=[pref + "_msk", pref + "_fac"], writes=[pref + "_scp"])
            P.op(eng, lambda e: e.tensor_mul(rs_t[:, 0:nb], rs_t[:, 0:nb], fac_rs()),
                 reads=[pref + "_rsp0", pref + "_fac"], writes=[pref + "_rsp"])
        else:
            P.op(eng, lambda e: e.tensor_scalar(sc_t[:, 0:nb], sc_t[:, 0:nb],
                                                cvec[:, 0:1], None, AluOpType.mult),
                 reads=[pref + "_msk", "cvec"], writes=[pref + "_scp"])
            P.op(eng, lambda e: e.tensor_scalar(rs_t[:, 0:nb], rs_t[:, 0:nb],
                                                cvec[:, 1:2], None, AluOpType.mult),
                 reads=[pref + "_rsp0", "cvec"], writes=[pref + "_rsp"])
        rsv = lambda: rs_t[:, 0:nb].rearrange("p (b o) -> p b o", o=1).broadcast_to((128, nb, 32))
        scv = lambda: sc_t[:, 0:nb].rearrange("p (b o) -> p b o", o=1).broadcast_to((128, nb, 32))
        P.op(eng, lambda e: e.tensor_tensor(xs_f().rearrange("p (b e) -> p b e", e=32),
                                            src_f().rearrange("p (b e) -> p b e", e=32),
                                            rsv(), AluOpType.mult),
             reads=extra_reads + [pref + "_rsp"], writes=[pref + "_xs"])
        P.op(eng, lambda e: e.tensor_scalar(xs_f(), xs_f(), 6.0, -6.0,
                                            AluOpType.min, AluOpType.max),
             reads=[pref + "_xs"], writes=[pref + "_xs"])
        P.op(eng, lambda e: e.tensor_scalar(mg_f().bitcast(I32), xs_f().bitcast(I32),
                                            EXPMASK, None, AluOpType.bitwise_and),
             reads=[pref + "_xs"], writes=[pref + "_mg"])
        P.op(eng, lambda e: e.tensor_scalar(mg_f(), mg_f(), 1.0, 6291456.0,
                                            AluOpType.max, AluOpType.mult),
             reads=[pref + "_mg"], writes=[pref + "_mg"])
        P.op(eng, lambda e: e.tensor_add(xs_f(), xs_f(), mg_f()),
             reads=[pref + "_xs", pref + "_mg"], writes=[pref + "_xs"])
        P.op(eng, lambda e: e.tensor_sub(xs_f(), xs_f(), mg_f()),
             reads=[pref + "_xs", pref + "_mg"], writes=[pref + "_xs"])
        P.op(eng, lambda e: e.tensor_tensor(dst_f().rearrange("p (b e) -> p b e", e=32),
                                            xs_f().rearrange("p (b e) -> p b e", e=32),
                                            scv(), AluOpType.mult),
             reads=[pref + "_xs", pref + "_scp"], writes=[pref + "_dst"] + list(dst_keys))

    if debug_taps:
        dbg_xn2 = nc.dram_tensor("dbg_xn2", [128, D], F32, kind="ExternalOutput")
    for i in range(NT):
        slot = i % 2
        P.op("vector", lambda e, i=i: e.tensor_scalar(xnt[:], xt[i][:],
                                                      negm[:, i:i + 1],
                                                      r_own[:, i:i + 1],
                                                      AluOpType.add, AluOpType.mult),
             reads=[("xt", i), "negm", "r_own", "q_dst"], writes=["xn", "qTr"])
        xreads = ["xn"]
        if debug_taps and i == NT - 1:
            P.op("sync", lambda e: e.dma_start(out=dbg_xn2[:, :], in_=xnt[:]),
                 reads=["xn"], writes=["xn_tap"], dma="dbg")
            xreads = ["xn", "xn_tap"]
        quant_chain("vector", lambda: xnt[:], lambda s=slot: xqt[s][:],
                    amax, qs_rs, qs_sc, lambda: xnt[:], lambda: mg_t[:],
                    D, "q", xreads, dst_keys=[("xqt", slot)])
        P.op("gpsimd", lambda e, i=i, slot=slot: e.dma_start(
            out=xq_dram[i * 128:(i + 1) * 128, :], in_=xqt[slot][:]),
            reads=["q_dst"], writes=[("xq_dram", i)], dma="xq")

    P.op("scalar", lambda e: e.activation(wfac[:], dwt[:], AF.Exp, scale=8.0),
         reads=["dwt"], writes=["wexp"])
    P.op("vector", lambda e: e.tensor_scalar(wfac[:], wfac[:], 1.0, None, AluOpType.add),
         reads=["wexp"], writes=["wexp"])
    P.op("vector", lambda e: e.reciprocal(wrfac[:], wfac[:]),
         reads=["wexp"], writes=["wrec"])
    P.op("vector", lambda e: e.tensor_scalar(dwt[:], wrfac[:], -2.0, 1.0,
                                             AluOpType.mult, AluOpType.add),
         reads=["wrec"], writes=["wtanh"])
    P.op("scalar", lambda e: e.activation(wfac[:], dwt[:], AF.Exp,
                                          scale=0.5 * LN2, bias=lnb_t[:, 0:1]),
         reads=["wtanh", "lnb0"], writes=["wfac4"])
    P.op("scalar", lambda e: e.activation(wrfac[:], dwt[:], AF.Exp,
                                          scale=-0.5 * LN2, bias=lnb_t[:, 1:2]),
         reads=["wtanh", "lnb1"], writes=["wrfac4"])
    for t in range(4):
        P.op("gpsimd", lambda e, t=t: e.dma_start(out=wt_in[:], in_=w_half[t * 128:(t + 1) * 128, :]),
             reads=["w_dst"], writes=["wt_in"] + WQ_ALIAS, dma="wl")
        P.op("gpsimd", lambda e, t=t: e.engine_nop(), reads=["wfac4", "wrfac4"], writes=["w_fac"])
        quant_chain("vector", lambda: wt_in[:], lambda: wt_in[:],
                    wamax, wrs, wsc, lambda: wt_in[:], lambda: wmg[:],
                    D, "w", ["wt_in"],
                    fac_mul=lambda t=t: wfac[:, t * NB:(t + 1) * NB],
                    fac_rs=lambda t=t: wrfac[:, t * NB:(t + 1) * NB])
        P.op("vector", lambda e: e.tensor_copy(wq_bf[:], wt_in[:]),
             reads=["w_dst"], writes=["wq_bf", "bias"])
        P.op("gpsimd", lambda e, sem, t=t: cond2(
            e, sem, lambda e2: e2.dma_start(out=wq_sh[t * 128:(t + 1) * 128, :], in_=wq_bf[:]),
            lambda e2: e2.dma_start(out=wq_sh[512 + t * 128:512 + (t + 1) * 128, :], in_=wq_bf[:])), reads=["wq_bf", "part_reg"], writes=[("wq_sh", t)], manual_sem=True)
    P.op("gpsimd", lambda e, sem: cond2(
        e, sem, lambda e2: e2.dma_start(out=wq_flag[0:1, :], in_=onesu[:]),
        lambda e2: e2.dma_start(out=wq_flag[1:2, :], in_=onesu[:])),
        reads=["onesu", "part_reg"] + [("wq_sh", t) for t in range(4)], writes=["wq_flag_w"],
        manual_sem=True)
    spin("wq", wq_flag[0:1, 0:1], wq_flag[1:2, 0:1], K=60)

    if stage <= 3:
        bailout()
        return nc

    if stage != 62:
        for d in range(8):
            P.op("sync", lambda e, d=d: e.dma_start(out=xqT_strip(d),
                                                    in_=xq_dram[:, d * 128:(d + 1) * 128],
                                                    transpose=True),
                 reads=[("xq_dram", i) for i in range(NT)] + [("pbuf_pv", b) for b in range(3)],
                 writes=XQT_KEYS, dma="xqT")
    if stage != 61:
        P.op("sync", lambda e: e.dma_start(out=wq_loc[:, :], in_=wq_sh[:, :]),
             reads=[("bar", "wq")], writes=["wq_loc"], dma="xqT")
        for d in range(8):
            P.op("sync", lambda e, d=d: e.dma_start(out=wqT_strip(d),
                                                    in_=wq_loc[:, d * 128:(d + 1) * 128],
                                                    transpose=True),
                 reads=["wq_loc", ("pbuf_pv", 2)],
                 writes=WQT_KEYS, dma="xqT")

    if stage in (6, 61, 62):
        bailout()
        return nc
    y_marks = []
    for u in range(NT if stage != 7 else 1):
        for oc in range(2):
            pslot = oc % 2
            ps_ = psA if pslot == 0 else psB
            pkey = PSA if pslot == 0 else PSB

            def ffn(e, u=u, oc=oc, ps_=ps_):
                last = None
                for d in range(8):
                    e.ldweights(xqT_strip(d)[:, u * 128:(u + 1) * 128])
                    last = e.matmul(ps_[:, 0:512], xqT_strip(d)[:, u * 128:(u + 1) * 128],
                                    wqT_strip(d)[:, oc * 512:(oc + 1) * 512],
                                    start=(d == 0), stop=False)
                last = e.matmul(ps_[:, 0:512], ones_row[0:1, :],
                                bias_bf[0:1, oc * 512:(oc + 1) * 512], start=False, stop=True)
                return last
            P.op("tensor", ffn, reads=XQT_KEYS + WQT_KEYS + ["ones_row", "bias_bf"],
                 writes=[pkey(0)])
            gch = lambda oc=oc: gt[:, oc * 512:(oc + 1) * 512]
            P.op("scalar", lambda e, oc=oc, ps_=ps_: e.activation(
                gch(oc), ps_[:, 0:512], AF.Exp, scale=-1.0),
                reads=[pkey(0), "gt_used"], writes=[("gt_e", oc)])
            P.op("scalar", lambda e, oc=oc: e.activation(
                gch(oc), gch(oc), AF.Ln, bias=onep[:]),
                reads=[("gt_e", oc), "onep"], writes=[("gt_l", oc)])
            P.op("scalar", lambda e, oc=oc: e.activation(
                gch(oc), gch(oc), AF.Exp, scale=-1.0),
                reads=[("gt_l", oc)], writes=[("gt_s", oc)])
            P.op("vector", lambda e, oc=oc, ps_=ps_: e.tensor_tensor(
                gch(oc), gch(oc), ps_[:, 0:512], AluOpType.mult),
                reads=[("gt_s", oc), pkey(0)], writes=[("gt", oc), ("xTs", 0)])

        def gate(e, u=u):
            sview = stok[:, u * H:(u + 1) * H].rearrange("p (o h) -> p o h", o=1)\
                .broadcast_to((128, 16, H))
            return e.scalar_tensor_tensor(ytt[:].rearrange("p (o h) -> p o h", o=16),
                                          gt[:].rearrange("p (o h) -> p o h", o=16),
                                          1.0, sview, AluOpType.mult, AluOpType.mult)
        P.op("vector", gate, reads=[("gt", 0), ("gt", 1), "stok"],
             writes=["ytt", "gt_used", ("xTs", 1)])
        P.op("vector", lambda e, u=u: e.tensor_add(ytt[:], ytt[:], xt[u][:]),
             reads=["ytt", ("xt", u)], writes=["ytt"])
        ym = P.op("sync", lambda e, u=u: e.dma_start(out=y_out[u * 128:(u + 1) * 128, :], in_=ytt[:]),
                  reads=["ytt"], writes=[("y_out", u)], dma="yo")
        y_marks.append(ym)

    if debug_taps:
        dbg_kv = nc.dram_tensor("dbg_kv", [256, 2 * R], BF16, kind="ExternalOutput")
        dbg_z = nc.dram_tensor("dbg_z", [2 * NG * 128, KPG], F32, kind="ExternalOutput")
        dbg_sf = nc.dram_tensor("dbg_sf", [64, R], F32, kind="ExternalOutput")
        dbg_st = nc.dram_tensor("dbg_st", [128, NT * H], F32, kind="ExternalOutput")
        dbg_zm = nc.dram_tensor("dbg_zm", [128, KT], F32, kind="ExternalOutput")
        dbg_q = nc.dram_tensor("dbg_q", [128, R], BF16, kind="ExternalOutput")
        dbg_kd = nc.dram_tensor("dbg_kd", [128, S], BF16, kind="ExternalOutput")
        dbg_vt = nc.dram_tensor("dbg_vt", [128, KT * H], BF16, kind="ExternalOutput")
        dbg_xq = nc.dram_tensor("dbg_xq", [R, D], BF16, kind="ExternalOutput")
        dbg_wq = nc.dram_tensor("dbg_wq", [D, D], BF16, kind="ExternalOutput")
        dbg_g = nc.dram_tensor("dbg_g", [128, D], F32, kind="ExternalOutput")
        ym = P.op("sync", lambda e: e.dma_start(out=dbg_kv[:, :], in_=kv_sh[:, :]),
                  reads=["kv_sh_w", ("bar", "kv")], dma="dbg")
        y_marks.append(ym)
        ym = P.op("sync", lambda e: e.dma_start(
            out=dbg_z[:, :].rearrange("(a p) k -> a p k", p=128),
            in_=z_sh[:, :, :]), reads=[("z_sh", g) for g in range(NG)], dma="dbg")
        y_marks.append(ym)
        ym = P.op("sync", lambda e: e.dma_start(out=dbg_sf[:, :], in_=sfeat[:, :]),
                  reads=["sfeat"], dma="dbg")
        y_marks.append(ym)
        ym = P.op("sync", lambda e: e.dma_start(out=dbg_st[:, :], in_=stok[:, :]),
                  reads=["stok"], dma="dbg")
        y_marks.append(ym)
        ym = P.op("sync", lambda e: e.dma_start(out=dbg_zm[:, :], in_=zmine[:, :]),
                  reads=[("zmine", g) for g in range(NG)], dma="dbg")
        y_marks.append(ym)
        ym = P.op("sync", lambda e: e.dma_start(out=dbg_q[:, :], in_=qTo[:, :]),
                  reads=["qTo"], dma="dbg")
        y_marks.append(ym)
        ym = P.op("sync", lambda e: e.dma_start(out=dbg_kd[:, :], in_=kTd[:, :]),
                  reads=["kTd_m", "kTd_p"], dma="dbg")
        y_marks.append(ym)
        ym = P.op("sync", lambda e: e.dma_start(out=dbg_vt[:, :], in_=vtok[:, :]),
                  reads=["vtok"], dma="dbg")
        y_marks.append(ym)
        ym = P.op("sync", lambda e: e.dma_start(out=dbg_xq[:, :], in_=xq_dram[:, :]),
                  reads=[("xq_dram", i) for i in range(NT)], dma="dbg")
        y_marks.append(ym)
        ym = P.op("sync", lambda e: e.dma_start(out=dbg_wq[:, :], in_=wq_sh[:, :]),
                  reads=[("bar", "wq")], dma="dbg")
        y_marks.append(ym)
        ym = P.op("sync", lambda e: e.dma_start(out=dbg_g[:, :], in_=gt[:, :]),
                  reads=[("gt", 0), ("gt", 1)], dma="dbg")
        y_marks.append(ym)
        dbg_cv = nc.dram_tensor("dbg_cv", [128, 8], F32, kind="ExternalOutput")
        dbg_qs = nc.dram_tensor("dbg_qs", [128, 3 * NB], F32, kind="ExternalOutput")
        dbg_xn = nc.dram_tensor("dbg_xn", [128, D], F32, kind="ExternalOutput")
        ym = P.op("sync", lambda e: e.dma_start(out=dbg_cv[:, 0:4], in_=cact[:, :]),
                  reads=["cact1"], dma="dbg")
        y_marks.append(ym)
        ym = P.op("sync", lambda e: e.dma_start(out=dbg_cv[:, 4:6], in_=cvec[:, :]),
                  reads=["cvec"], dma="dbg")
        y_marks.append(ym)
        ym = P.op("sync", lambda e: e.dma_start(out=dbg_cv[:, 6:8], in_=lnb_t[:, :]),
                  reads=["lnb0", "lnb1"], dma="dbg")
        y_marks.append(ym)
        ym = P.op("sync", lambda e: e.dma_start(out=dbg_qs[:, 0:NB], in_=amax[:, :]),
                  reads=["q_amax"], dma="dbg")
        y_marks.append(ym)
        ym = P.op("sync", lambda e: e.dma_start(out=dbg_qs[:, NB:2 * NB], in_=qs_rs[:, :]),
                  reads=["q_rsp"], dma="dbg")
        y_marks.append(ym)
        ym = P.op("sync", lambda e: e.dma_start(out=dbg_qs[:, 2 * NB:3 * NB], in_=qs_sc[:, :]),
                  reads=["q_scp"], dma="dbg")
        y_marks.append(ym)
        ym = P.op("sync", lambda e: e.dma_start(out=dbg_xn[:, :], in_=xnt[:]),
                  reads=["xn"], dma="dbg")
        y_marks.append(ym)
        dbg_mr = nc.dram_tensor("dbg_mr", [128, 4 * NT], F32, kind="ExternalOutput")
        ym = P.op("sync", lambda e: e.dma_start(out=dbg_mr[:, 0:NT], in_=m_own[:, :]),
                  reads=["m_own"], dma="dbg")
        y_marks.append(ym)
        ym = P.op("sync", lambda e: e.dma_start(out=dbg_mr[:, NT:2 * NT], in_=r_own[:, :]),
                  reads=["r_own"], dma="dbg")
        y_marks.append(ym)
        ym = P.op("sync", lambda e: e.dma_start(out=dbg_mr[:, 2 * NT:3 * NT], in_=negmr[:, :]),
                  reads=["negmr"], dma="dbg")
        y_marks.append(ym)
        ym = P.op("sync", lambda e: e.dma_start(out=dbg_mr[:, 3 * NT:4 * NT], in_=var_t[:, :]),
                  reads=["var"], dma="dbg")
        y_marks.append(ym)

    def final_wait(e):
        for sname, val in y_marks:
            e.wait_ge(P.sems[sname], val)
        return None
    P.raw("sync", final_wait)
    P.emit()
    return nc


def kernel(L0_attn_x, L0_attn_Wq, L0_attn_Wk, L0_attn_Wv,
           L0_ffn_weight, L0_ffn_bias, L0_ffn_delta_w, L0_ffn_delta_act):
    x = np.ascontiguousarray(np.asarray(L0_attn_x, dtype=np.float32))
    B, S, _ = x.shape
    R = S // 4
    nc = build_nc(R=R)
    ident = np.eye(128, dtype=bf16)
    wq = np.ascontiguousarray(np.asarray(L0_attn_Wq, dtype=np.float32))
    wk = np.ascontiguousarray(np.asarray(L0_attn_Wk, dtype=np.float32))
    wv = np.ascontiguousarray(np.asarray(L0_attn_Wv, dtype=np.float32))
    w = np.asarray(L0_ffn_weight, dtype=np.float32)
    dw = np.asarray(L0_ffn_delta_w, dtype=np.float32)
    bias = np.ascontiguousarray(np.asarray(L0_ffn_bias, dtype=np.float32)[None, :])
    dact = np.ascontiguousarray(
        np.broadcast_to(np.asarray(L0_ffn_delta_act, dtype=np.float32).reshape(1, 1), (128, 1)))
    in_maps = []
    for c in range(8):
        b = c // 4
        qr = c % 4
        q2 = qr ^ 2
        par = c % 2
        xb = x[b]
        xTb = np.ascontiguousarray(xb.T)
        xT_half = np.ascontiguousarray(
            np.concatenate([xTb[:, qr * R:(qr + 1) * R], xTb[:, q2 * R:(q2 + 1) * R]], axis=1))
        in_maps.append({
            "x_own": np.ascontiguousarray(xb[qr * R:(qr + 1) * R]),
            "x_red": np.ascontiguousarray(xb[q2 * R:(q2 + 1) * R]),
            "xT_half": xT_half,
            "wq_in": wq, "wk_in": wk, "wv_in": wv,
            "w_half": np.ascontiguousarray(w[par * 512:(par + 1) * 512]),
            "dw_half": np.ascontiguousarray(dw[par * 512:(par + 1) * 512]),
            "bias_in": bias, "dact_in": dact,
            "ident_bf": ident,
            "par_in": np.array([[par]], dtype=np.uint32),
        })
    res = run_bass_kernel_spmd(nc, in_maps, core_ids=list(range(8)))
    y = np.empty((B, S, D), dtype=np.float32)
    for c in range(8):
        b = c // 4
        qr = c % 4
        y[b, qr * R:(qr + 1) * R] = res.results[c]["y_out"]
    return y


# version 07
# speedup vs baseline: 1.0671x; 1.0671x over previous
import numpy as np
import ml_dtypes
import concourse.bass as bass
import concourse.mybir as mybir
from concourse.bass_utils import run_bass_kernel_spmd
from concourse.alu_op_type import AluOpType
from contextlib import ExitStack

F32 = mybir.dt.float32
BF16 = mybir.dt.bfloat16
I32 = mybir.dt.int32
U32 = mybir.dt.uint32
AF = mybir.ActivationFunctionType
bf16 = ml_dtypes.bfloat16

D = 1024
H = 64
NB = D // 32
LN2 = float(np.log(2.0))
EXPB = -20.0
EPS = 1e-5
TINY = 1e-30
MAGIC_C1 = (22 << 23) | 0x400000
EXPMASK = 0x7F800000
ONEBITS = 127 << 23


class Prog:
    ENGINES = ("sync", "tensor", "vector", "scalar", "gpsimd")

    def __init__(self, nc):
        self.nc = nc
        self.ops = {e: [] for e in self.ENGINES}
        self.seq = {e: 0 for e in self.ENGINES}
        self.sems = {}
        self.dcnt = {}
        self.state = {}
        self.water = {}
        self._es = ExitStack()
        self.POOL_N = 12
        self.EPOOL_N = 8
        self.pool_rr2 = {}
        self.epool_rr = {e: 0 for e in self.ENGINES}

    def sem(self, name):
        if name not in self.sems:
            self.sems[name] = self._es.enter_context(self.nc.semaphore(name))
        return self.sems[name]

    def _wait(self, eng, marker):
        if marker is None:
            return None
        sname, val = marker
        key = (eng, sname)
        if self.water.get(key, 0) >= val:
            return None
        self.water[key] = val
        return (self.sems[sname], val)

    def op(self, eng, fn, reads=(), writes=(), dma=None, manual_sem=None):
        waits = []

        def add(m):
            w = self._wait(eng, m)
            if w:
                waits.append(w)

        for b in reads:
            st = self.state.get(b)
            if st and st["w"]:
                add(st["w"])
        for b in writes:
            st = self.state.get(b)
            if st:
                if st["w"]:
                    add(st["w"])
                for r in st["r"]:
                    add(r)
        if dma is not None or manual_sem is not None:
            pk = "sw" if eng == "gpsimd" else "hw"
            j = self.pool_rr2.get(pk, 0) % self.POOL_N
            self.pool_rr2[pk] = self.pool_rr2.get(pk, 0) + 1
            sname = f"dp_{pk}{j}"
            self.sem(sname)
            prev = self.dcnt.get(sname, 0)
            if prev:
                w = self._wait(eng, (sname, prev))
                if w:
                    waits.append(w)
            self.dcnt[sname] = prev + 16
            marker = (sname, prev + 16)
            if manual_sem is not None:
                realfn = fn
                semh = self.sems[sname]
                fn = (lambda realfn=realfn, semh=semh: (lambda e: realfn(e, semh)))()
                inc = None
            else:
                inc = (self.sems[sname], 16)
        else:
            j = self.epool_rr[eng] % self.EPOOL_N
            self.epool_rr[eng] += 1
            sname = f"e_{eng}{j}"
            self.sem(sname)
            prev = self.dcnt.get(sname, 0)
            if prev:
                w = self._wait(eng, (sname, prev))
                if w:
                    waits.append(w)
            self.dcnt[sname] = prev + 1
            marker = (sname, prev + 1)
            inc = (self.sems[sname], 1)
        for b in reads:
            self.state.setdefault(b, {"w": None, "r": []})["r"].append(marker)
        for b in writes:
            self.state[b] = {"w": marker, "r": []}
        self.ops[eng].append((waits, fn, inc))
        return marker

    def raw(self, eng, fn):
        self.ops[eng].append(([], fn, None))

    def emit(self):
        nc = self.nc
        with nc.Block() as block:
            for ename, deco in (("sync", block.sync), ("tensor", block.tensor),
                                ("vector", block.vector), ("scalar", block.scalar),
                                ("gpsimd", block.gpsimd)):
                ops = self.ops[ename]

                def make(ops_list):
                    def body(eng):
                        for waits, fn, inc in ops_list:
                            for sem, val in waits:
                                eng.wait_ge(sem, val)
                            if fn is None:
                                continue
                            inst = fn(eng)
                            if inc is not None:
                                if inst is None:
                                    eng.sem_inc(inc[0], inc[1])
                                else:
                                    inst.then_inc(inc[0], inc[1])
                    return body
                deco(make(ops))


KILL = {}


def build_nc(R=2048, KPG=4, debug_taps=False, stage=9):
    S = 4 * R
    NT = R // 128
    KT = S // 128
    NG = KT // KPG
    assert NG % 2 == 0
    CH = min(512, R)
    NCH = R // CH
    ECH = min(512, R)
    NEC = R // ECH
    CPE = ECH // CH
    COLH = min(2048, R)
    NCOLH = (2 * R) // COLH
    CC_N = COLH // 512 if COLH >= 512 else 1
    CW = min(512, COLH)

    nc = bass.Bass("TRN2", target_bir_lowering=False, debug=False,
                   detect_race_conditions=False)

    x_own = nc.dram_tensor("x_own", [R, D], F32, kind="ExternalInput")
    x_red = nc.dram_tensor("x_red", [R, D], F32, kind="ExternalInput")
    xT_half = nc.dram_tensor("xT_half", [D, 2 * R], F32, kind="ExternalInput")
    wq_in = nc.dram_tensor("wq_in", [D, H], F32, kind="ExternalInput")
    wk_in = nc.dram_tensor("wk_in", [D, H], F32, kind="ExternalInput")
    wv_in = nc.dram_tensor("wv_in", [D, H], F32, kind="ExternalInput")
    w_half = nc.dram_tensor("w_half", [D // 2, D], F32, kind="ExternalInput")
    dw_half = nc.dram_tensor("dw_half", [D // 2, NB], F32, kind="ExternalInput")
    bias_in = nc.dram_tensor("bias_in", [1, D], F32, kind="ExternalInput")
    dact_in = nc.dram_tensor("dact_in", [128, 1], F32, kind="ExternalInput")
    ident_bf = nc.dram_tensor("ident_bf", [128, 128], BF16, kind="ExternalInput")
    par_in = nc.dram_tensor("par_in", [1, 1], U32, kind="ExternalInput")
    y_out = nc.dram_tensor("y_out", [R, D], F32, kind="ExternalOutput")

    xq_dram = nc.dram_tensor("xq_dram", [R, D], BF16)
    mr_dram = nc.dram_tensor("mr_dram", [64, 128], BF16)
    delay_dram = nc.dram_tensor("delay_dram", [128, 8192], F32)
    delay_dram2 = nc.dram_tensor("delay_dram2", [128, 8192], F32)
    hello_flag = nc.dram_tensor("hello_flag", [2, 16], U32, addr_space="Shared")
    kv_sh = nc.dram_tensor("kv_sh", [2 * 128, 2 * R], BF16, addr_space="Shared")
    rr_sh = nc.dram_tensor("rr_sh", [2 * 128, 2 * NT], F32, addr_space="Shared")
    kv_flag = nc.dram_tensor("kv_flag", [2, 16], U32, addr_space="Shared")
    z_sh = nc.dram_tensor("z_sh", [2 * NG, 128, KPG], F32, addr_space="Shared")
    z_flag = nc.dram_tensor("z_flag", [2 * NG, 16], U32, addr_space="Shared")
    wq_sh = nc.dram_tensor("wq_sh", [D, D], BF16, addr_space="Shared")
    wq_loc = nc.dram_tensor("wq_loc", [D, D], BF16)
    wq_flag = nc.dram_tensor("wq_flag", [2, 16], U32, addr_space="Shared")

    sb = lambda n, s, d: nc.alloc_sbuf_tensor(n, s, d)

    def sb_at(name, shape, dtype, host):
        off = nc.lookup_mloc(host).addr
        return nc.alloc_sbuf_tensor_at(name, shape, dtype, offset=off)

    xt = [sb(f"xt{i}", [128, D], F32) for i in range(NT)]
    stat = sb("stat", [128, 12], F32)
    stat2 = sb("stat2", [128, 2], F32)
    m_own = sb("m_own", [128, NT], F32)
    r_own = sb("r_own", [128, NT], F32)
    m_red = sb("m_red", [128, NT], F32)
    r_red = sb("r_red", [128, NT], F32)
    negmr = sb("negmr", [128, NT], F32)
    negm = sb("negm", [128, NT], F32)
    var_t = sb("var_t", [128, NT], F32)
    mr_pack = sb("mr_pack", [128, 64], BF16)
    mrT = sb("mrT", [64, 128], BF16)
    m_row = sb("m_row", [1, 2 * R], BF16)
    rrow_tmp = sb("rrow_tmp", [1, 2 * R], BF16)
    r_rowq = sb("r_rowq", [64, 2 * R], BF16)
    r_cols = sb("r_cols", [128, 2 * KT], F32)
    idb = sb("idb", [128, 128], BF16)
    identf = sb("identf", [64, 64], F32)
    ones_row = sb("ones_row", [1, 128], BF16)
    ones_col = sb("ones_col", [128, 1], BF16)
    wkv_s = sb("wkv_s", [128, 8 * 128], BF16)
    wq_s = sb("wq_s", [128, 8 * 64], BF16)
    cw = sb("cw", [1, 192], BF16)
    xTs = [sb(f"xTs{i}", [128, COLH], BF16) for i in range(2)]
    kvT = sb("kvT", [128, 2 * R], BF16)
    qTo = sb("qTo", [128, R], BF16)
    qTr = sb("qTr", [128, R], BF16)
    kTd = sb("kTd", [128, S], BF16)
    vtok = sb("vtok", [128, KT * H], BF16)
    if 2 * S >= 5 * D + D:
        wt_in = kTd[:, 0:2 * D].bitcast(F32)
        wmg = kTd[:, 2 * D:4 * D].bitcast(F32)
        wq_bf = kTd[:, 4 * D:5 * D]
        WQ_ALIAS = ["kTd_m", "kTd_p"]
    else:
        wt_in = sb("wt_in", [128, D], F32)
        wmg = sb("wmg", [128, D], F32)
        wq_bf = sb("wq_bf", [128, D], BF16)
        WQ_ALIAS = []
    pbuf = [sb(f"pbuf{i}", [128, KPG * R], BF16) for i in range(3)]
    trash = sb("trash", [128, ECH], BF16)
    delay_t = trash[:, 0:ECH].bitcast(F32)
    zacc = sb("zacc", [128, KT * 8], F32)
    zmine = sb("zmine", [128, KT], F32)
    zpart = sb("zpart", [128, KPG], F32)
    zfull = sb("zfull", [128, KPG], F32)
    vzs = sb("vzs", [128, KPG], F32)
    vpp = [sb(f"vpp{i}", [128, KPG * H], BF16) for i in range(2)]
    stok = qTo[:, :].bitcast(F32)[:, 0:NT * H]
    if R >= 2 * D:
        xnt = qTr[:, :].bitcast(F32)[:, 0:D]
    else:
        xnt = sb("xnt", [128, D], F32)
    xqt = [sb(f"xqt{i}", [128, D], F32) for i in range(2)]
    xrt = xqt
    amax = sb("amax", [128, NB], F32)
    qs_rs = sb("qs_rs", [128, NB], F32)
    qs_sc = sb("qs_sc", [128, NB], F32)
    if COLH >= 2 * D:
        mg_t = xTs[1][:, :].bitcast(F32)
    else:
        mg_t = sb("mg_t", [128, D], F32)
    wamax = sb("wamax", [128, NB], F32)
    wrs = sb("wrs", [128, NB], F32)
    wsc = sb("wsc", [128, NB], F32)
    wfac = sb("wfac", [128, 4 * NB], F32)
    wrfac = sb("wrfac", [128, 4 * NB], F32)
    dwt = sb("dwt", [128, 4 * NB], F32)
    cact = sb("cact", [128, 4], F32)
    cvec = sb("cvec", [128, 2], F32)
    bias_bf = sb("bias_bf", [1, D], BF16)
    if COLH >= 2 * D:
        gt = xTs[0][:, :].bitcast(F32)
        ytt = xTs[1][:, :].bitcast(F32)
    else:
        gt = sb("gt", [128, D], F32)
        ytt = sb("ytt", [128, D], F32)
    flagt = sb("flagt", [1, 4], U32)
    expb_t = sb("expb_t", [128, 1], F32)
    onep = sb("onep", [128, 1], F32)
    lnb_t = sb("lnb_t", [128, 2], F32)
    onesu = sb("onesu", [1, 16], U32)
    part = sb("part", [1, 1], U32)

    sfeat = kvT[0:64, :].bitcast(F32)
    vstg = pbuf[0][0:64, 0:2 * R]

    if 2 * KPG * R >= 8 * R and KPG * R >= 8 * D:
        def xqT_strip(d):
            b = d // KPG
            o = (d % KPG) * R
            return pbuf[b][:, o:o + R]

        def wqT_strip(d):
            return pbuf[2][:, d * D:(d + 1) * D]
        XQT_KEYS = [("pbuf", 0), ("pbuf", 1)]
        WQT_KEYS = [("pbuf", 2)]
    else:
        xqTs_t = sb("xqTs", [128, 8 * R], BF16)
        wqTs_t = sb("wqTs", [128, 8 * D], BF16)

        def xqT_strip(d):
            return xqTs_t[:, d * R:(d + 1) * R]

        def wqT_strip(d):
            return wqTs_t[:, d * D:(d + 1) * D]
        XQT_KEYS = ["xqTs_t"]
        WQT_KEYS = ["wqTs_t"]

    psA = nc.alloc_psum_tensor("psA", [128, 1024], F32)
    psB = nc.alloc_psum_tensor("psB", [128, 1024], F32)
    out_ps = nc.alloc_psum_tensor("out_ps", [128, R], F32)
    PSA = lambda h: ("psA", h)
    PSB = lambda h: ("psB", h)

    P = Prog(nc)
    preg = nc.alloc_register(mybir.EngineType.Pool, "preg")
    spreg = nc.alloc_register(mybir.EngineType.Pool, "spreg")
    RH = bass.RegisterHandles
    def bailout():
        marks = []
        for u in range(NT):
            m = P.op("sync", lambda e, u=u: e.dma_start(out=y_out[u * 128:(u + 1) * 128, :],
                                                        in_=xt[u][:]),
                     reads=[("xt", u)], dma="yo")
            marks.append(m)

        def fw(e):
            for sname, val in marks:
                e.wait_ge(P.sems[sname], val)
            return None
        P.raw("sync", fw)
        P.emit()


    def cond2(e, sem, fn0, fn1):
        with nc.If_cmp(RH((preg,)), 0, "IS_EQ"):
            fn0(e).then_inc(sem, 16)
        nc.end_ifs()
        with nc.If_cmp(RH((preg,)), 0, "IS_NE"):
            fn1(e).then_inc(sem, 16)
        nc.end_ifs()
        return None

    spreg2 = nc.alloc_register(mybir.EngineType.Pool, "spreg2")

    def spin(tag, flag_row0, flag_row1, K=3, EXP=8, CW=8192, XTRA=0):
        bar = P.sem("bar_" + tag)
        pol = P.sem("pol")
        dly = P.sem("dly")
        p0 = P.dcnt.get("pol", 0)
        NPOLL = K + EXP + XTRA

        def f(e):
            cnt = p0
            for k in range(NPOLL):
                nd = 0
                if k >= K:
                    nd = min(1 << (k - K), 64)
                for _ in range(nd):
                    with nc.If_cmp(bass.RegisterHandles((spreg,)), 2, "IS_LT"):
                        e.dma_start(out=delay_dram2[:, 0:CW],
                                    in_=delay_dram[:, 0:CW]).then_inc(dly, 16)
                    nc.end_ifs()
                e.dma_start(out=flagt[0:1, 0:1], in_=flag_row0).then_inc(pol, 16)
                e.dma_start(out=flagt[0:1, 1:2], in_=flag_row1).then_inc(pol, 16)
                cnt += 32
                e.wait_ge(pol, cnt)
                e.reg_load(spreg, flagt[0:1, 0:1])
                e.reg_load(spreg2, flagt[0:1, 1:2])
                e.reg_alu(spreg, spreg, spreg2, AluOpType.add)
                with nc.If_cmp(bass.RegisterHandles((spreg,)), 2, "IS_GE"):
                    e.sem_inc(bar, 1)
                nc.end_ifs()
            e.wait_ge(bar, 1)
            return None
        P.dcnt["pol"] = p0 + 32 * NPOLL
        P.op("gpsimd", f, reads=["part_reg", "flagt", "delay_dram"],
             writes=[("bar", tag), "flagt"])
        return ("bar_" + tag, 1)

    P.op("sync", lambda e: e.dma_start(out=idb[:], in_=ident_bf[:, :]), writes=["idb"], dma="cst")
    P.op("gpsimd", lambda e: e.dma_start(out=bias_bf[:], in_=bias_in[:, :]), writes=["bias_bf"], dma="cst2")
    P.op("sync", lambda e: e.dma_start(out=cact[:, 0:1], in_=dact_in[:, :]), writes=["cact0"], dma="cst")
    P.op("sync", lambda e: e.dma_start(out=part[:], in_=par_in[:, :]), writes=["part"], dma="cst")
    if stage <= -2:
        bailout()
        return nc

    P.op("gpsimd", lambda e: e.memset(onesu[:], 1), writes=["onesu"])
    P.op("gpsimd", lambda e: e.memset(ones_row[:], 1.0), writes=["ones_row"])
    P.op("gpsimd", lambda e: e.memset(ones_col[:], 1.0), writes=["ones_col"])
    P.op("gpsimd", lambda e: e.memset(expb_t[:], EXPB), writes=["expb"])
    P.op("gpsimd", lambda e: e.memset(onep[:], 1.0), writes=["onep"])
    if KILL.get("delayinit", False) is False:
        P.op("gpsimd", lambda e: e.memset(trash[:].bitcast(F32), 0.0), writes=["trash"])
        DCH_W = ECH // 2
        for ci in range(8192 // DCH_W):
            P.op("gpsimd", lambda e, ci=ci: e.dma_start(
                out=delay_dram[:, ci * DCH_W:(ci + 1) * DCH_W], in_=delay_t),
                reads=["trash"], writes=["delay_dram"], dma="gc")
    P.op("gpsimd", lambda e: e.memset(lnb_t[:, 0:1], -2.0 * LN2), writes=["lnb0"])
    P.op("gpsimd", lambda e: e.memset(lnb_t[:, 1:2], 1.0 * LN2), writes=["lnb1"])

    def loadreg(e):
        e.reg_load(preg, part[0:1, 0:1])
        e.reg_load(spreg, part[0:1, 0:1])
        e.reg_load(spreg2, part[0:1, 0:1])
        return None
    P.op("gpsimd", loadreg, reads=["part"], writes=["part_reg"])

    P.op("gpsimd", lambda e, sem: cond2(
        e, sem, lambda e2: e2.dma_start(out=hello_flag[0:1, :], in_=onesu[:]),
        lambda e2: e2.dma_start(out=hello_flag[1:2, :], in_=onesu[:])),
        reads=["onesu", "part_reg"], writes=["hello_w"], manual_sem=True)
    spin("hello", hello_flag[0:1, 0:1], hello_flag[1:2, 0:1], K=3, EXP=10, XTRA=6)

    P.op("gpsimd", lambda e: e.dma_start(
        out=wkv_s[:].rearrange("p (t m) -> p t m", t=8)[:, :, 0:64],
        in_=wk_in[:, :].rearrange("(t p) h -> p t h", p=128)), writes=["wkv_k"], dma="gc")
    P.op("gpsimd", lambda e: e.dma_start(
        out=wkv_s[:].rearrange("p (t m) -> p t m", t=8)[:, :, 64:128],
        in_=wv_in[:, :].rearrange("(t p) h -> p t h", p=128)), writes=["wkv_v"], dma="gc")
    P.op("gpsimd", lambda e: e.dma_start(
        out=wq_s[:].rearrange("p (t h) -> p t h", t=8),
        in_=wq_in[:, :].rearrange("(t p) h -> p t h", p=128)), writes=["wq_s"], dma="gc")
    P.op("gpsimd", lambda e: e.dma_start(
        out=dwt[:].rearrange("p (t b) -> p t b", t=4),
        in_=dw_half[:, :].rearrange("(t p) b -> p t b", p=128)), writes=["dwt"], dma="gc")

    if stage <= -1:
        bailout()
        return nc

    P.op("scalar", lambda e: e.activation(cact[:, 2:3], cact[:, 0:1], AF.Exp, scale=8.0),
         reads=["cact0"], writes=["cact2"])
    P.op("vector", lambda e: e.tensor_scalar(cact[:, 2:3], cact[:, 2:3], 1.0, None,
                                             AluOpType.add),
         reads=["cact2"], writes=["cact2"])
    P.op("vector", lambda e: e.reciprocal(cact[:, 3:4], cact[:, 2:3]),
         reads=["cact2"], writes=["cact3"])
    P.op("vector", lambda e: e.tensor_scalar(cact[:, 1:2], cact[:, 3:4], -2.0, 1.0,
                                             AluOpType.mult, AluOpType.add),
         reads=["cact3"], writes=["cact1"])
    P.op("scalar", lambda e: e.activation(cvec[:, 0:1], cact[:, 1:2], AF.Exp,
                                          scale=0.5 * LN2, bias=lnb_t[:, 0:1]),
         reads=["cact1", "lnb0"], writes=["cvec"])
    P.op("scalar", lambda e: e.activation(cvec[:, 1:2], cact[:, 1:2], AF.Exp,
                                          scale=-0.5 * LN2, bias=lnb_t[:, 1:2]),
         reads=["cact1", "lnb1", "cvec"], writes=["cvec"])

    def cw_mms(e):
        last = None
        for t in range(8):
            last = e.matmul(psB[0:1, 512:640], ones_col[:, :], wkv_s[:, t * 128:(t + 1) * 128],
                            start=(t == 0), stop=(t == 7))
        for t in range(8):
            last = e.matmul(psB[0:1, 640:704], ones_col[:, :], wq_s[:, t * 64:(t + 1) * 64],
                            start=(t == 0), stop=(t == 7))
        return last
    P.op("tensor", cw_mms, reads=["wkv_k", "wkv_v", "wq_s", "ones_col"], writes=[PSB(1)])
    P.op("vector", lambda e: e.tensor_copy(cw[0:1, :], psB[0:1, 512:704]),
         reads=[PSB(1)], writes=["cw"])

    for i in range(NT):
        P.op("sync", lambda e, i=i: e.dma_start(out=xt[i][:], in_=x_own[i * 128:(i + 1) * 128, :]),
             writes=[("xt", i)], dma="xo")
    for i in range(NT):
        P.op("vector", lambda e, i=i: e.bn_stats(stat[:, 0:6], xt[i][:, 0:512]),
             reads=[("xt", i)], writes=["stat6a"])
        P.op("vector", lambda e, i=i: e.bn_stats(stat[:, 6:12], xt[i][:, 512:1024]),
             reads=[("xt", i)], writes=["stat6b"])
        P.op("vector", lambda e: e.bn_aggr(stat2[:, 0:2], stat[:, 0:12]),
             reads=["stat6a", "stat6b"], writes=["stat"])
        P.op("vector", lambda e, i=i: e.tensor_copy(m_own[:, i:i + 1], stat2[:, 0:1]),
             reads=["stat"], writes=["m_own"])
        P.op("vector", lambda e, i=i: e.tensor_scalar(var_t[:, i:i + 1], stat2[:, 1:2], EPS, None,
                                                      AluOpType.add),
             reads=["stat"], writes=["var"])
    P.op("scalar", lambda e: e.activation(var_t[:], var_t[:], AF.Ln), reads=["var"], writes=["var"])
    P.op("scalar", lambda e: e.activation(r_own[:], var_t[:], AF.Exp, scale=-0.5),
         reads=["var"], writes=["r_own"])
    for i in range(NT):
        P.op("scalar", lambda e, i=i: e.dma_start(out=xrt[i % 2][:], in_=x_red[i * 128:(i + 1) * 128, :]),
             writes=[("xqt", i % 2)], dma="xr")
        P.op("vector", lambda e, i=i: e.bn_stats(stat[:, 0:6], xrt[i % 2][:, 0:512]),
             reads=[("xqt", i % 2)], writes=["stat6a"])
        P.op("vector", lambda e, i=i: e.bn_stats(stat[:, 6:12], xrt[i % 2][:, 512:1024]),
             reads=[("xqt", i % 2)], writes=["stat6b"])
        P.op("vector", lambda e: e.bn_aggr(stat2[:, 0:2], stat[:, 0:12]),
             reads=["stat6a", "stat6b"], writes=["stat"])
        P.op("vector", lambda e, i=i: e.tensor_copy(m_red[:, i:i + 1], stat2[:, 0:1]),
             reads=["stat"], writes=["m_red"])
        P.op("vector", lambda e, i=i: e.tensor_scalar(var_t[:, i:i + 1], stat2[:, 1:2], EPS, None,
                                                      AluOpType.add),
             reads=["stat"], writes=["varr"])
    P.op("scalar", lambda e: e.activation(var_t[:], var_t[:], AF.Ln), reads=["varr"], writes=["varr"])
    P.op("scalar", lambda e: e.activation(r_red[:], var_t[:], AF.Exp, scale=-0.5),
         reads=["varr"], writes=["r_red"])
    P.op("vector", lambda e: e.tensor_mul(negmr[:], m_own[:], r_own[:]),
         reads=["m_own", "r_own"], writes=["negmr"])
    P.op("vector", lambda e: e.tensor_scalar(negmr[:], negmr[:], -1.0, None, AluOpType.mult),
         reads=["negmr"], writes=["negmr"])
    P.op("vector", lambda e: e.tensor_scalar(negm[:], m_own[:], -1.0, None, AluOpType.mult),
         reads=["m_own"], writes=["negm"])

    P.op("vector", lambda e: e.memset(mr_pack[:], 0.0), writes=["mrp"])
    P.op("vector", lambda e: e.tensor_scalar(mr_pack[:, 0:NT], m_own[:], -1.0, None, AluOpType.mult),
         reads=["m_own", "mrp"], writes=["mrp"])
    P.op("vector", lambda e: e.tensor_scalar(mr_pack[:, 16:16 + NT], m_red[:], -1.0, None,
                                             AluOpType.mult),
         reads=["m_red", "mrp"], writes=["mrp"])
    P.op("vector", lambda e: e.tensor_copy(mr_pack[:, 32:32 + NT], r_own[:]),
         reads=["r_own", "mrp"], writes=["mrp"])
    P.op("vector", lambda e: e.tensor_copy(mr_pack[:, 48:48 + NT], r_red[:]),
         reads=["r_red", "mrp"], writes=["mrp"])
    P.op("tensor", lambda e: e.transpose(psB[0:64, 0:64].bitcast(BF16), mr_pack[:, 0:64], idb[:, :]),
         reads=["mrp", "idb"], writes=[PSB(0)])
    P.op("vector", lambda e: e.tensor_copy(mrT[:, :], psB[0:64, 0:64].bitcast(BF16)),
         reads=[PSB(0)], writes=["mrT"])
    P.op("gpsimd", lambda e: e.dma_start(out=mr_dram[:, :], in_=mrT[:, :]),
         reads=["mrT"], writes=["mr_dram"], dma="mr")
    P.op("sync", lambda e: e.dma_start(
        out=m_row[:, :].rearrange("o (h t p) -> o h t p", h=2, t=NT),
        in_=mr_dram[0:32, :].rearrange("(o h t) p -> o h t p", o=1, h=2)[:, :, 0:NT]),
        reads=["mr_dram"], writes=["m_row"], dma="mrr")
    P.op("sync", lambda e: e.dma_start(
        out=rrow_tmp[:, :].rearrange("o (h t p) -> o h t p", h=2, t=NT),
        in_=mr_dram[32:64, :].rearrange("(o h t) p -> o h t p", o=1, h=2)[:, :, 0:NT]),
        reads=["mr_dram"], writes=["rrow_tmp"], dma="mrr")
    for ci in range(2 * R // 512):
        half = ci % 2
        P.op("tensor", lambda e, ci=ci, half=half: e.matmul(
            psA[0:64, half * 512:half * 512 + 512], ones_row[0:1, 0:64],
            rrow_tmp[0:1, ci * 512:(ci + 1) * 512], start=True, stop=True),
            reads=["rrow_tmp", "ones_row"], writes=[PSA(half)])
        P.op("vector", lambda e, ci=ci, half=half: e.tensor_copy(
            r_rowq[:, ci * 512:(ci + 1) * 512], psA[0:64, half * 512:half * 512 + 512]),
            reads=[PSA(half)], writes=["r_rowq"])
    P.op("vector", lambda e: e.tensor_copy(r_cols[:, 0:NT], r_own[:]),
         reads=["r_own"], writes=["r_cols_m"])
    P.op("vector", lambda e: e.tensor_copy(r_cols[:, NT:2 * NT], r_red[:]),
         reads=["r_red", "r_cols_m"], writes=["r_cols_m"])

    if stage <= 0:
        bailout()
        return nc

    for ch in range(NCOLH):
        buf = ch % 2
        c0 = ch * COLH
        for d in range(8):
            P.op("gpsimd", lambda e, d=d, c0=c0, buf=buf: e.dma_start(
                out=xTs[buf][:], in_=xT_half[d * 128:(d + 1) * 128, c0:c0 + COLH]),
                writes=[("xTs", buf)], dma="xT")

            def mmkv(e, d=d, buf=buf):
                last = None
                for cc in range(CC_N):
                    last = e.matmul(out_ps[:, cc * 512:cc * 512 + CW],
                                    wkv_s[:, d * 128:(d + 1) * 128],
                                    xTs[buf][:, cc * 512:cc * 512 + CW],
                                    start=(d == 0), stop=False)
                for cc in range(CC_N):
                    tgt, off = (psA, (cc % 2) * 512) if cc < 2 else (psB, (cc % 2) * 512)
                    last = e.matmul(tgt[0:64, off:off + CW],
                                    wq_s[:, d * 64:(d + 1) * 64],
                                    xTs[buf][:, cc * 512:cc * 512 + CW],
                                    start=(d == 0), stop=False)
                return last
            P.op("tensor", mmkv, reads=[("xTs", buf), "wkv_k", "wkv_v", "wq_s"],
                 writes=[("outps",), PSA(0), PSA(1), PSB(0), PSB(1)])

        def fix(e, c0=c0):
            last = None
            for cc in range(CC_N):
                last = e.matmul(out_ps[:, cc * 512:cc * 512 + CW], cw[0:1, 0:128],
                                m_row[0:1, c0 + cc * 512:c0 + cc * 512 + CW],
                                start=False, stop=True)
                tgt, off = (psA, (cc % 2) * 512) if cc < 2 else (psB, (cc % 2) * 512)
                last = e.matmul(tgt[0:64, off:off + CW], cw[0:1, 128:192],
                                m_row[0:1, c0 + cc * 512:c0 + cc * 512 + CW],
                                start=False, stop=True)
            return last
        P.op("tensor", fix, reads=["cw", "m_row", ("outps",), PSA(0), PSA(1), PSB(0), PSB(1)],
             writes=[("outps",), PSA(0), PSA(1), PSB(0), PSB(1)])
        for cc in range((COLH + 511) // 512):
            wdt = min(512, COLH - cc * 512)
            P.op("vector", lambda e, c0=c0, cc=cc, wdt=wdt: e.tensor_copy(
                kvT[:, c0 + cc * 512:c0 + cc * 512 + wdt], out_ps[:, cc * 512:cc * 512 + wdt]),
                reads=[("outps",)], writes=["kvT"])

        for cc in range(CC_N):
            tgt, off = (psA, (cc % 2) * 512) if cc < 2 else (psB, (cc % 2) * 512)
            gc0 = c0 + cc * 512
            dst = qTo if gc0 < R else qTr
            do = gc0 % R
            pk = PSA(cc % 2) if cc < 2 else PSB(cc % 2)
            P.op("vector", lambda e, tgt=tgt, off=off, dst=dst, do=do, gc0=gc0:
                 e.scalar_tensor_tensor(dst[0:64, do:do + CW], tgt[0:64, off:off + CW],
                                        1.0, r_rowq[0:64, gc0:gc0 + CW],
                                        AluOpType.mult, AluOpType.mult),
                 reads=[pk, "r_rowq"], writes=["qT0"])
    P.op("vector", lambda e: e.tensor_copy(qTo[64:128, :], qTo[0:64, :]),
         reads=["qT0"], writes=["qTo"])
    P.op("vector", lambda e: e.tensor_copy(qTr[64:128, :], qTr[0:64, :]),
         reads=["qT0", "qTo"], writes=["qTr"])

    if stage <= -1:
        pass
    if stage == 5:
        bailout()
        return nc

    P.op("gpsimd", lambda e, sem: cond2(
        e, sem, lambda e2: e2.dma_start(out=kv_sh[0:128, :], in_=kvT[:]),
        lambda e2: e2.dma_start(out=kv_sh[128:256, :], in_=kvT[:])),
        reads=["kvT", "part_reg"], writes=["kv_sh_w"], manual_sem=True)
    P.op("gpsimd", lambda e, sem: cond2(
        e, sem, lambda e2: e2.dma_start(out=rr_sh[0:128, :], in_=r_cols[:, 0:2 * NT]),
        lambda e2: e2.dma_start(out=rr_sh[128:256, :], in_=r_cols[:, 0:2 * NT])),
        reads=["r_cols_m", "part_reg"], writes=["rr_sh_w"], manual_sem=True)
    P.op("gpsimd", lambda e, sem: cond2(
        e, sem, lambda e2: e2.dma_start(out=kv_flag[0:1, :], in_=onesu[:]),
        lambda e2: e2.dma_start(out=kv_flag[1:2, :], in_=onesu[:])),
        reads=["onesu", "part_reg", "kv_sh_w", "rr_sh_w"], writes=["kv_flag_w"],
        manual_sem=True)
    kv_bar = spin("kv", kv_flag[0:1, 0:1], kv_flag[1:2, 0:1])
    P.op("gpsimd", lambda e, sem: cond2(
        e, sem, lambda e2: e2.dma_start(out=kTd[0:64, 2 * R:4 * R], in_=kv_sh[128:192, :]),
        lambda e2: e2.dma_start(out=kTd[0:64, 2 * R:4 * R], in_=kv_sh[0:64, :])),
        reads=[("bar", "kv"), "part_reg"], writes=["kTd_p"], manual_sem=True)
    P.op("gpsimd", lambda e, sem: cond2(
        e, sem, lambda e2: e2.dma_start(out=kTd[64:128, 2 * R:4 * R], in_=kv_sh[128:192, :]),
        lambda e2: e2.dma_start(out=kTd[64:128, 2 * R:4 * R], in_=kv_sh[0:64, :])),
        reads=[("bar", "kv"), "part_reg", "kTd_p"], writes=["kTd_p"], manual_sem=True)
    P.op("gpsimd", lambda e, sem: cond2(
        e, sem, lambda e2: e2.dma_start(out=vstg, in_=kv_sh[192:256, :]),
        lambda e2: e2.dma_start(out=vstg, in_=kv_sh[64:128, :])),
        reads=[("bar", "kv"), "part_reg"], writes=[("pbuf", 0)], manual_sem=True)
    P.op("gpsimd", lambda e, sem: cond2(
        e, sem, lambda e2: e2.dma_start(out=r_cols[:, 2 * NT:4 * NT], in_=rr_sh[128:256, :]),
        lambda e2: e2.dma_start(out=r_cols[:, 2 * NT:4 * NT], in_=rr_sh[0:128, :])),
        reads=[("bar", "kv"), "part_reg", "r_cols_m"], writes=["r_cols"], manual_sem=True)
    P.op("vector", lambda e: e.tensor_copy(kTd[0:64, 0:2 * R], kvT[0:64, :]),
         reads=["kvT"], writes=["kTd_m"])
    P.op("vector", lambda e: e.tensor_copy(kTd[64:128, 0:2 * R], kvT[0:64, :]),
         reads=["kvT", "kTd_m"], writes=["kTd_m"])
    for t in range(KT):
        if t < KT // 2:
            src = kvT[64:128, t * 128:(t + 1) * 128]
            rk = ["kvT"]
            ident = idb[64:128, 64:128]
        else:
            src = vstg[:, (t - KT // 2) * 128:(t - KT // 2 + 1) * 128]
            rk = [("pbuf", 0)]
            ident = idb[0:64, 0:64]
        half = t % 2
        P.op("tensor", lambda e, src=src, half=half, ident=ident: e.transpose(
            psA[:, half * 512:half * 512 + 32].bitcast(BF16), src, ident),
            reads=rk + ["idb"], writes=[PSA(half)])
        P.op("vector", lambda e, t=t, half=half: e.tensor_copy(
            vtok[:, t * H:(t + 1) * H], psA[:, half * 512:half * 512 + 32].bitcast(BF16)),
            reads=[PSA(half)], writes=["vtok"])

    if stage <= 1:
        bailout()
        return nc

    P.op("vector", lambda e: e.memset(zacc[:], 0.0), writes=[("zacc", k) for k in range(KT * 8)])
    gorder = []
    for i in range(NG // 2):
        gorder += [i, i + NG // 2]

    def strip(tile, red, pslot, pb, pcol):
        ps_ = psA if pslot == 0 else psB
        pkey = PSA if pslot == 0 else PSB
        rg = 0 if pslot == 0 else 64
        q = qTr if red else qTo
        qk = "qTr" if red else "qTo"
        kslice = kTd[rg:rg + 64, tile * 128:(tile + 1) * 128]
        for ec in range(NEC):
            half = ec % 2
            def mms(e, ec=ec, kslice=kslice, ps_=ps_, rg=rg, q=q, half=half):
                last = e.ldweights(kslice, tile_position=(rg, 0))
                c = ec * ECH
                last = e.matmul(ps_[:, half * 512:half * 512 + ECH], kslice,
                                q[rg:rg + 64, c:c + ECH],
                                start=True, stop=True, tile_position=(rg, 0))
                return last
            P.op("tensor", mms, reads=["kTd_m", "kTd_p", qk], writes=[pkey(half)])
            acc_i = tile * 8 + (4 if red else 0) + ec
            if red:
                outap = trash[:, 0:ECH]
                wk = ["trash"]
            else:
                outap = pbuf[pb][:, pcol * R + ec * ECH: pcol * R + (ec + 1) * ECH]
                wk = [("pbuf", pb)]
            P.op("scalar", lambda e, outap=outap, ps_=ps_, acc_i=acc_i, tile=tile, half=half:
                 e.activation(outap, ps_[:, half * 512:half * 512 + ECH], AF.Exp,
                              bias=expb_t[:], scale=r_cols[:, tile:tile + 1],
                              accum_out=zacc[:, acc_i:acc_i + 1]),
                 reads=[pkey(half), "r_cols", "expb"],
                 writes=wk + [("zacc", acc_i)])

    def emit_group_strips(si, g):
        pb = si % 3
        for j in range(KPG):
            strip(g * KPG + j, False, j % 2, pb, j)
        for j in range(KPG):
            strip(g * KPG + j, True, j % 2, pb, j)

        def zsum(e, g=g):
            return e.reduce_sum(zmine[:, g * KPG:(g + 1) * KPG],
                                zacc[:, g * KPG * 8:(g + 1) * KPG * 8]
                                .rearrange("p (t f) -> p t f", f=8),
                                axis=mybir.AxisListType.X)
        P.op("vector", zsum,
             reads=[("zacc", g * KPG * 8 + k) for k in range(KPG * 8)], writes=[("zmine", g)])
        P.op("gpsimd", lambda e, sem, g=g: cond2(
            e, sem, lambda e2: e2.dma_start(out=z_sh[g:g + 1, :, :].rearrange("o p k -> p (o k)"),
                                       in_=zmine[:, g * KPG:(g + 1) * KPG]),
            lambda e2: e2.dma_start(out=z_sh[NG + g:NG + g + 1, :, :].rearrange("o p k -> p (o k)"),
                                    in_=zmine[:, g * KPG:(g + 1) * KPG])),
            reads=[("zmine", g), "part_reg"], writes=[("z_sh", g)], manual_sem=True)
        P.op("gpsimd", lambda e, sem, g=g: cond2(
            e, sem, lambda e2: e2.dma_start(out=z_flag[g:g + 1, :], in_=onesu[:]),
            lambda e2: e2.dma_start(out=z_flag[NG + g:NG + g + 1, :], in_=onesu[:])),
            reads=["onesu", "part_reg", ("z_sh", g)], writes=[("z_flag", g)], manual_sem=True)

    def emit_group_pv(si, g, is_first, is_last):
        pb = si % 3
        gp = (g + NG // 2) % NG
        spin(f"z{g}", z_flag[gp:gp + 1, 0:1], z_flag[NG + gp:NG + gp + 1, 0:1])
        P.op("gpsimd", lambda e, sem, gp=gp: cond2(
            e, sem, lambda e2: e2.dma_start(out=zpart[:, :],
                                       in_=z_sh[NG + gp:NG + gp + 1, :, :].rearrange("o p k -> p (o k)")),
            lambda e2: e2.dma_start(out=zpart[:, :],
                                    in_=z_sh[gp:gp + 1, :, :].rearrange("o p k -> p (o k)"))), reads=[("bar", f"z{g}"), "part_reg"], writes=["zpart"], manual_sem=True)
        P.op("vector", lambda e, g=g: e.tensor_add(zfull[:, :], zmine[:, g * KPG:(g + 1) * KPG],
                                                   zpart[:, :]),
             reads=[("zmine", g), "zpart"], writes=["zfull"])
        P.op("vector", lambda e: e.reciprocal(zfull[:], zfull[:]), reads=["zfull"], writes=["zfull"])
        P.op("vector", lambda e, g=g: e.tensor_mul(vzs[:, :], zfull[:, :],
                                                   r_cols[:, g * KPG:(g + 1) * KPG]),
             reads=["zfull", "r_cols"], writes=["vzs"])
        vslot = si % 2

        for j in range(KPG):
            t = g * KPG + j
            P.op("vector", lambda e, j=j, t=t, vslot=vslot:
                 e.tensor_scalar(vpp[vslot][:, j * H:(j + 1) * H], vtok[:, t * H:(t + 1) * H],
                                 vzs[:, j:j + 1], None, AluOpType.mult),
                 reads=["vtok", "vzs"], writes=[("vpp", vslot)])

        def pv(e, pb=pb, vslot=vslot, is_first=is_first, is_last=is_last):
            last = None
            for j in range(0, KPG, 2):
                e.ldweights(vpp[vslot][:, j * H:(j + 1) * H], tile_position=(0, 0))
                e.ldweights(vpp[vslot][:, (j + 1) * H:(j + 2) * H], tile_position=(0, 64))
                for cc in range(NCH):
                    last = e.matmul(out_ps[0:64, cc * CH:(cc + 1) * CH],
                                    vpp[vslot][:, j * H:(j + 1) * H],
                                    pbuf[pb][:, j * R + cc * CH:j * R + (cc + 1) * CH],
                                    start=(is_first and j == 0),
                                    stop=(is_last and j == KPG - 2),
                                    tile_position=(0, 0))
                    last = e.matmul(out_ps[64:128, cc * CH:(cc + 1) * CH],
                                    vpp[vslot][:, (j + 1) * H:(j + 2) * H],
                                    pbuf[pb][:, (j + 1) * R + cc * CH:(j + 1) * R + (cc + 1) * CH],
                                    start=(is_first and j == 0),
                                    stop=(is_last and j == KPG - 2),
                                    tile_position=(0, 64), skip_group_check=True)
            return last
        P.op("tensor", pv, reads=[("vpp", vslot), ("pbuf", pb)],
             writes=[("outps",), ("pbuf_pv", pb)])

    for si, g in enumerate(gorder):
        emit_group_strips(si, g)
        if si >= 1:
            emit_group_pv(si - 1, gorder[si - 1], si - 1 == 0, si - 1 == len(gorder) - 1)
    emit_group_pv(len(gorder) - 1, gorder[-1], len(gorder) - 1 == 0, True)

    for cc in range(R // 512 if R >= 512 else 1):
        wdt = min(512, R)
        P.op("vector", lambda e, cc=cc, wdt=wdt: e.tensor_copy(
            sfeat[:, cc * 512:cc * 512 + wdt], out_ps[0:64, cc * 512:cc * 512 + wdt]),
            reads=[("outps",)], writes=["sfeat", "kvT"])
        P.op("vector", lambda e, cc=cc, wdt=wdt: e.scalar_tensor_tensor(
            sfeat[:, cc * 512:cc * 512 + wdt], out_ps[64:128, cc * 512:cc * 512 + wdt], 1.0,
            sfeat[:, cc * 512:cc * 512 + wdt], AluOpType.mult, AluOpType.add),
            reads=[("outps",), "sfeat"], writes=["sfeat"])
    P.op("vector", lambda e: e.tensor_copy(identf[:, :], idb[0:64, 0:64]),
         reads=["idb"], writes=["identf"])
    for u in range(NT):
        half = u % 2
        P.op("tensor", lambda e, u=u, half=half: e.transpose(
            psA[:, half * 512:half * 512 + 64], sfeat[:, u * 128:(u + 1) * 128], identf[:, :]),
            reads=["sfeat", "identf"], writes=[PSA(half)])
        P.op("vector", lambda e, u=u, half=half: e.tensor_copy(
            stok[:, u * H:(u + 1) * H], psA[:, half * 512:half * 512 + 64]),
            reads=[PSA(half)], writes=["stok", "qTo"])

    if stage <= 2:
        bailout()
        return nc

    def quant_chain(eng, src_f, dst_f, am_t, rs_t, sc_t, xs_f, mg_f, n, pref, extra_reads,
                    fac_mul=None, fac_rs=None, dst_keys=()):
        nb = n // 32
        P.op("vector", lambda e: e.reduce_max(am_t[:, 0:nb],
                                              src_f().rearrange("p (b e) -> p b e", e=32),
                                              axis=mybir.AxisListType.X,
                                              apply_absolute_value=True),
             reads=extra_reads, writes=[pref + "_amax"])
        P.op(eng, lambda e: e.tensor_scalar(am_t[:, 0:nb], am_t[:, 0:nb], TINY, None,
                                            AluOpType.max),
             reads=[pref + "_amax"], writes=[pref + "_amax"])
        P.op(eng, lambda e: e.tensor_scalar(sc_t[:, 0:nb].bitcast(I32),
                                            am_t[:, 0:nb].bitcast(I32), EXPMASK, None,
                                            AluOpType.bitwise_and),
             reads=[pref + "_amax"], writes=[pref + "_msk"])
        P.op(eng, lambda e: e.tensor_scalar(rs_t[:, 0:nb].bitcast(I32),
                                            sc_t[:, 0:nb].bitcast(I32), EXPMASK, None,
                                            AluOpType.bitwise_xor),
             reads=[pref + "_msk"], writes=[pref + "_rsp0"])
        if fac_mul is not None:
            P.op(eng, lambda e: e.tensor_mul(sc_t[:, 0:nb], sc_t[:, 0:nb], fac_mul()),
                 reads=[pref + "_msk", pref + "_fac"], writes=[pref + "_scp"])
            P.op(eng, lambda e: e.tensor_mul(rs_t[:, 0:nb], rs_t[:, 0:nb], fac_rs()),
                 reads=[pref + "_rsp0", pref + "_fac"], writes=[pref + "_rsp"])
        else:
            P.op(eng, lambda e: e.tensor_scalar(sc_t[:, 0:nb], sc_t[:, 0:nb],
                                                cvec[:, 0:1], None, AluOpType.mult),
                 reads=[pref + "_msk", "cvec"], writes=[pref + "_scp"])
            P.op(eng, lambda e: e.tensor_scalar(rs_t[:, 0:nb], rs_t[:, 0:nb],
                                                cvec[:, 1:2], None, AluOpType.mult),
                 reads=[pref + "_rsp0", "cvec"], writes=[pref + "_rsp"])
        rsv = lambda: rs_t[:, 0:nb].rearrange("p (b o) -> p b o", o=1).broadcast_to((128, nb, 32))
        scv = lambda: sc_t[:, 0:nb].rearrange("p (b o) -> p b o", o=1).broadcast_to((128, nb, 32))
        P.op(eng, lambda e: e.tensor_tensor(xs_f().rearrange("p (b e) -> p b e", e=32),
                                            src_f().rearrange("p (b e) -> p b e", e=32),
                                            rsv(), AluOpType.mult),
             reads=extra_reads + [pref + "_rsp"], writes=[pref + "_xs"])
        P.op(eng, lambda e: e.tensor_scalar(xs_f(), xs_f(), 6.0, -6.0,
                                            AluOpType.min, AluOpType.max),
             reads=[pref + "_xs"], writes=[pref + "_xs"])
        P.op(eng, lambda e: e.tensor_scalar(mg_f().bitcast(I32), xs_f().bitcast(I32),
                                            EXPMASK, None, AluOpType.bitwise_and),
             reads=[pref + "_xs"], writes=[pref + "_mg"])
        P.op(eng, lambda e: e.tensor_scalar(mg_f(), mg_f(), 1.0, 6291456.0,
                                            AluOpType.max, AluOpType.mult),
             reads=[pref + "_mg"], writes=[pref + "_mg"])
        P.op(eng, lambda e: e.tensor_add(xs_f(), xs_f(), mg_f()),
             reads=[pref + "_xs", pref + "_mg"], writes=[pref + "_xs"])
        P.op(eng, lambda e: e.tensor_sub(xs_f(), xs_f(), mg_f()),
             reads=[pref + "_xs", pref + "_mg"], writes=[pref + "_xs"])
        P.op(eng, lambda e: e.tensor_tensor(dst_f().rearrange("p (b e) -> p b e", e=32),
                                            xs_f().rearrange("p (b e) -> p b e", e=32),
                                            scv(), AluOpType.mult),
             reads=[pref + "_xs", pref + "_scp"], writes=[pref + "_dst"] + list(dst_keys))

    if debug_taps:
        dbg_xn2 = nc.dram_tensor("dbg_xn2", [128, D], F32, kind="ExternalOutput")
    for i in range(NT):
        slot = i % 2
        P.op("vector", lambda e, i=i: e.tensor_scalar(xnt[:], xt[i][:],
                                                      negm[:, i:i + 1],
                                                      r_own[:, i:i + 1],
                                                      AluOpType.add, AluOpType.mult),
             reads=[("xt", i), "negm", "r_own", "q_dst"], writes=["xn", "qTr"])
        xreads = ["xn"]
        if debug_taps and i == NT - 1:
            P.op("sync", lambda e: e.dma_start(out=dbg_xn2[:, :], in_=xnt[:]),
                 reads=["xn"], writes=["xn_tap"], dma="dbg")
            xreads = ["xn", "xn_tap"]
        quant_chain("vector", lambda: xnt[:], lambda s=slot: xqt[s][:],
                    amax, qs_rs, qs_sc, lambda: xnt[:], lambda: mg_t[:],
                    D, "q", xreads, dst_keys=[("xqt", slot)])
        P.op("gpsimd", lambda e, i=i, slot=slot: e.dma_start(
            out=xq_dram[i * 128:(i + 1) * 128, :], in_=xqt[slot][:]),
            reads=["q_dst"], writes=[("xq_dram", i)], dma="xq")

    P.op("scalar", lambda e: e.activation(wfac[:], dwt[:], AF.Exp, scale=8.0),
         reads=["dwt"], writes=["wexp"])
    P.op("vector", lambda e: e.tensor_scalar(wfac[:], wfac[:], 1.0, None, AluOpType.add),
         reads=["wexp"], writes=["wexp"])
    P.op("vector", lambda e: e.reciprocal(wrfac[:], wfac[:]),
         reads=["wexp"], writes=["wrec"])
    P.op("vector", lambda e: e.tensor_scalar(dwt[:], wrfac[:], -2.0, 1.0,
                                             AluOpType.mult, AluOpType.add),
         reads=["wrec"], writes=["wtanh"])
    P.op("scalar", lambda e: e.activation(wfac[:], dwt[:], AF.Exp,
                                          scale=0.5 * LN2, bias=lnb_t[:, 0:1]),
         reads=["wtanh", "lnb0"], writes=["wfac4"])
    P.op("scalar", lambda e: e.activation(wrfac[:], dwt[:], AF.Exp,
                                          scale=-0.5 * LN2, bias=lnb_t[:, 1:2]),
         reads=["wtanh", "lnb1"], writes=["wrfac4"])
    for t in range(4):
        P.op("gpsimd", lambda e, t=t: e.dma_start(out=wt_in[:], in_=w_half[t * 128:(t + 1) * 128, :]),
             reads=["w_dst"], writes=["wt_in"] + WQ_ALIAS, dma="wl")
        P.op("gpsimd", lambda e, t=t: e.engine_nop(), reads=["wfac4", "wrfac4"], writes=["w_fac"])
        quant_chain("vector", lambda: wt_in[:], lambda: wt_in[:],
                    wamax, wrs, wsc, lambda: wt_in[:], lambda: wmg[:],
                    D, "w", ["wt_in"],
                    fac_mul=lambda t=t: wfac[:, t * NB:(t + 1) * NB],
                    fac_rs=lambda t=t: wrfac[:, t * NB:(t + 1) * NB])
        P.op("vector", lambda e: e.tensor_copy(wq_bf[:], wt_in[:]),
             reads=["w_dst"], writes=["wq_bf", "bias"])
        P.op("gpsimd", lambda e, sem, t=t: cond2(
            e, sem, lambda e2: e2.dma_start(out=wq_sh[t * 128:(t + 1) * 128, :], in_=wq_bf[:]),
            lambda e2: e2.dma_start(out=wq_sh[512 + t * 128:512 + (t + 1) * 128, :], in_=wq_bf[:])), reads=["wq_bf", "part_reg"], writes=[("wq_sh", t)], manual_sem=True)
    P.op("gpsimd", lambda e, sem: cond2(
        e, sem, lambda e2: e2.dma_start(out=wq_flag[0:1, :], in_=onesu[:]),
        lambda e2: e2.dma_start(out=wq_flag[1:2, :], in_=onesu[:])),
        reads=["onesu", "part_reg"] + [("wq_sh", t) for t in range(4)], writes=["wq_flag_w"],
        manual_sem=True)
    spin("wq", wq_flag[0:1, 0:1], wq_flag[1:2, 0:1])

    if stage <= 3:
        bailout()
        return nc

    if stage != 62:
        for d in range(8):
            P.op("sync", lambda e, d=d: e.dma_start(out=xqT_strip(d),
                                                    in_=xq_dram[:, d * 128:(d + 1) * 128],
                                                    transpose=True),
                 reads=[("xq_dram", i) for i in range(NT)] + [("pbuf_pv", b) for b in range(3)],
                 writes=XQT_KEYS, dma="xqT")
    if stage != 61:
        P.op("sync", lambda e: e.dma_start(out=wq_loc[:, :], in_=wq_sh[:, :]),
             reads=[("bar", "wq")], writes=["wq_loc"], dma="xqT")
        for d in range(8):
            P.op("sync", lambda e, d=d: e.dma_start(out=wqT_strip(d),
                                                    in_=wq_loc[:, d * 128:(d + 1) * 128],
                                                    transpose=True),
                 reads=["wq_loc", ("pbuf_pv", 2)],
                 writes=WQT_KEYS, dma="xqT")

    if stage in (6, 61, 62):
        bailout()
        return nc
    y_marks = []
    for u in range(NT if stage != 7 else 1):
        for oc in range(2):
            pslot = oc % 2
            ps_ = psA if pslot == 0 else psB
            pkey = PSA if pslot == 0 else PSB

            def ffn(e, u=u, oc=oc, ps_=ps_):
                last = None
                for d in range(8):
                    e.ldweights(xqT_strip(d)[:, u * 128:(u + 1) * 128])
                    last = e.matmul(ps_[:, 0:512], xqT_strip(d)[:, u * 128:(u + 1) * 128],
                                    wqT_strip(d)[:, oc * 512:(oc + 1) * 512],
                                    start=(d == 0), stop=False)
                last = e.matmul(ps_[:, 0:512], ones_row[0:1, :],
                                bias_bf[0:1, oc * 512:(oc + 1) * 512], start=False, stop=True)
                return last
            P.op("tensor", ffn, reads=XQT_KEYS + WQT_KEYS + ["ones_row", "bias_bf"],
                 writes=[pkey(0)])
            gch = lambda oc=oc: gt[:, oc * 512:(oc + 1) * 512]
            P.op("scalar", lambda e, oc=oc, ps_=ps_: e.activation(
                gch(oc), ps_[:, 0:512], AF.Exp, scale=-1.0),
                reads=[pkey(0), "gt_used"], writes=[("gt_e", oc)])
            P.op("scalar", lambda e, oc=oc: e.activation(
                gch(oc), gch(oc), AF.Ln, bias=onep[:]),
                reads=[("gt_e", oc), "onep"], writes=[("gt_l", oc)])
            P.op("scalar", lambda e, oc=oc: e.activation(
                gch(oc), gch(oc), AF.Exp, scale=-1.0),
                reads=[("gt_l", oc)], writes=[("gt_s", oc)])
            P.op("vector", lambda e, oc=oc, ps_=ps_: e.tensor_tensor(
                gch(oc), gch(oc), ps_[:, 0:512], AluOpType.mult),
                reads=[("gt_s", oc), pkey(0)], writes=[("gt", oc), ("xTs", 0)])

        def gate(e, u=u):
            sview = stok[:, u * H:(u + 1) * H].rearrange("p (o h) -> p o h", o=1)\
                .broadcast_to((128, 16, H))
            return e.scalar_tensor_tensor(ytt[:].rearrange("p (o h) -> p o h", o=16),
                                          gt[:].rearrange("p (o h) -> p o h", o=16),
                                          1.0, sview, AluOpType.mult, AluOpType.mult)
        P.op("vector", gate, reads=[("gt", 0), ("gt", 1), "stok"],
             writes=["ytt", "gt_used", ("xTs", 1)])
        P.op("vector", lambda e, u=u: e.tensor_add(ytt[:], ytt[:], xt[u][:]),
             reads=["ytt", ("xt", u)], writes=["ytt"])
        ym = P.op("sync", lambda e, u=u: e.dma_start(out=y_out[u * 128:(u + 1) * 128, :], in_=ytt[:]),
                  reads=["ytt"], writes=[("y_out", u)], dma="yo")
        y_marks.append(ym)

    if debug_taps:
        dbg_kv = nc.dram_tensor("dbg_kv", [256, 2 * R], BF16, kind="ExternalOutput")
        dbg_z = nc.dram_tensor("dbg_z", [2 * NG * 128, KPG], F32, kind="ExternalOutput")
        dbg_sf = nc.dram_tensor("dbg_sf", [64, R], F32, kind="ExternalOutput")
        dbg_st = nc.dram_tensor("dbg_st", [128, NT * H], F32, kind="ExternalOutput")
        dbg_zm = nc.dram_tensor("dbg_zm", [128, KT], F32, kind="ExternalOutput")
        dbg_q = nc.dram_tensor("dbg_q", [128, R], BF16, kind="ExternalOutput")
        dbg_kd = nc.dram_tensor("dbg_kd", [128, S], BF16, kind="ExternalOutput")
        dbg_vt = nc.dram_tensor("dbg_vt", [128, KT * H], BF16, kind="ExternalOutput")
        dbg_xq = nc.dram_tensor("dbg_xq", [R, D], BF16, kind="ExternalOutput")
        dbg_wq = nc.dram_tensor("dbg_wq", [D, D], BF16, kind="ExternalOutput")
        dbg_g = nc.dram_tensor("dbg_g", [128, D], F32, kind="ExternalOutput")
        ym = P.op("sync", lambda e: e.dma_start(out=dbg_kv[:, :], in_=kv_sh[:, :]),
                  reads=["kv_sh_w", ("bar", "kv")], dma="dbg")
        y_marks.append(ym)
        ym = P.op("sync", lambda e: e.dma_start(
            out=dbg_z[:, :].rearrange("(a p) k -> a p k", p=128),
            in_=z_sh[:, :, :]), reads=[("z_sh", g) for g in range(NG)], dma="dbg")
        y_marks.append(ym)
        ym = P.op("sync", lambda e: e.dma_start(out=dbg_sf[:, :], in_=sfeat[:, :]),
                  reads=["sfeat"], dma="dbg")
        y_marks.append(ym)
        ym = P.op("sync", lambda e: e.dma_start(out=dbg_st[:, :], in_=stok[:, :]),
                  reads=["stok"], dma="dbg")
        y_marks.append(ym)
        ym = P.op("sync", lambda e: e.dma_start(out=dbg_zm[:, :], in_=zmine[:, :]),
                  reads=[("zmine", g) for g in range(NG)], dma="dbg")
        y_marks.append(ym)
        ym = P.op("sync", lambda e: e.dma_start(out=dbg_q[:, :], in_=qTo[:, :]),
                  reads=["qTo"], dma="dbg")
        y_marks.append(ym)
        ym = P.op("sync", lambda e: e.dma_start(out=dbg_kd[:, :], in_=kTd[:, :]),
                  reads=["kTd_m", "kTd_p"], dma="dbg")
        y_marks.append(ym)
        ym = P.op("sync", lambda e: e.dma_start(out=dbg_vt[:, :], in_=vtok[:, :]),
                  reads=["vtok"], dma="dbg")
        y_marks.append(ym)
        ym = P.op("sync", lambda e: e.dma_start(out=dbg_xq[:, :], in_=xq_dram[:, :]),
                  reads=[("xq_dram", i) for i in range(NT)], dma="dbg")
        y_marks.append(ym)
        ym = P.op("sync", lambda e: e.dma_start(out=dbg_wq[:, :], in_=wq_sh[:, :]),
                  reads=[("bar", "wq")], dma="dbg")
        y_marks.append(ym)
        ym = P.op("sync", lambda e: e.dma_start(out=dbg_g[:, :], in_=gt[:, :]),
                  reads=[("gt", 0), ("gt", 1)], dma="dbg")
        y_marks.append(ym)
        dbg_cv = nc.dram_tensor("dbg_cv", [128, 8], F32, kind="ExternalOutput")
        dbg_qs = nc.dram_tensor("dbg_qs", [128, 3 * NB], F32, kind="ExternalOutput")
        dbg_xn = nc.dram_tensor("dbg_xn", [128, D], F32, kind="ExternalOutput")
        ym = P.op("sync", lambda e: e.dma_start(out=dbg_cv[:, 0:4], in_=cact[:, :]),
                  reads=["cact1"], dma="dbg")
        y_marks.append(ym)
        ym = P.op("sync", lambda e: e.dma_start(out=dbg_cv[:, 4:6], in_=cvec[:, :]),
                  reads=["cvec"], dma="dbg")
        y_marks.append(ym)
        ym = P.op("sync", lambda e: e.dma_start(out=dbg_cv[:, 6:8], in_=lnb_t[:, :]),
                  reads=["lnb0", "lnb1"], dma="dbg")
        y_marks.append(ym)
        ym = P.op("sync", lambda e: e.dma_start(out=dbg_qs[:, 0:NB], in_=amax[:, :]),
                  reads=["q_amax"], dma="dbg")
        y_marks.append(ym)
        ym = P.op("sync", lambda e: e.dma_start(out=dbg_qs[:, NB:2 * NB], in_=qs_rs[:, :]),
                  reads=["q_rsp"], dma="dbg")
        y_marks.append(ym)
        ym = P.op("sync", lambda e: e.dma_start(out=dbg_qs[:, 2 * NB:3 * NB], in_=qs_sc[:, :]),
                  reads=["q_scp"], dma="dbg")
        y_marks.append(ym)
        ym = P.op("sync", lambda e: e.dma_start(out=dbg_xn[:, :], in_=xnt[:]),
                  reads=["xn"], dma="dbg")
        y_marks.append(ym)
        dbg_mr = nc.dram_tensor("dbg_mr", [128, 4 * NT], F32, kind="ExternalOutput")
        ym = P.op("sync", lambda e: e.dma_start(out=dbg_mr[:, 0:NT], in_=m_own[:, :]),
                  reads=["m_own"], dma="dbg")
        y_marks.append(ym)
        ym = P.op("sync", lambda e: e.dma_start(out=dbg_mr[:, NT:2 * NT], in_=r_own[:, :]),
                  reads=["r_own"], dma="dbg")
        y_marks.append(ym)
        ym = P.op("sync", lambda e: e.dma_start(out=dbg_mr[:, 2 * NT:3 * NT], in_=negmr[:, :]),
                  reads=["negmr"], dma="dbg")
        y_marks.append(ym)
        ym = P.op("sync", lambda e: e.dma_start(out=dbg_mr[:, 3 * NT:4 * NT], in_=var_t[:, :]),
                  reads=["var"], dma="dbg")
        y_marks.append(ym)

    def final_wait(e):
        for sname, val in y_marks:
            e.wait_ge(P.sems[sname], val)
        return None
    P.raw("sync", final_wait)
    P.emit()
    return nc


def kernel(L0_attn_x, L0_attn_Wq, L0_attn_Wk, L0_attn_Wv,
           L0_ffn_weight, L0_ffn_bias, L0_ffn_delta_w, L0_ffn_delta_act):
    x = np.ascontiguousarray(np.asarray(L0_attn_x, dtype=np.float32))
    B, S, _ = x.shape
    R = S // 4
    nc = build_nc(R=R)
    ident = np.eye(128, dtype=bf16)
    wq = np.ascontiguousarray(np.asarray(L0_attn_Wq, dtype=np.float32))
    wk = np.ascontiguousarray(np.asarray(L0_attn_Wk, dtype=np.float32))
    wv = np.ascontiguousarray(np.asarray(L0_attn_Wv, dtype=np.float32))
    w = np.asarray(L0_ffn_weight, dtype=np.float32)
    dw = np.asarray(L0_ffn_delta_w, dtype=np.float32)
    bias = np.ascontiguousarray(np.asarray(L0_ffn_bias, dtype=np.float32)[None, :])
    dact = np.ascontiguousarray(
        np.broadcast_to(np.asarray(L0_ffn_delta_act, dtype=np.float32).reshape(1, 1), (128, 1)))
    in_maps = []
    for c in range(8):
        b = c // 4
        qr = c % 4
        q2 = qr ^ 2
        par = c % 2
        xb = x[b]
        xTb = np.ascontiguousarray(xb.T)
        xT_half = np.ascontiguousarray(
            np.concatenate([xTb[:, qr * R:(qr + 1) * R], xTb[:, q2 * R:(q2 + 1) * R]], axis=1))
        in_maps.append({
            "x_own": np.ascontiguousarray(xb[qr * R:(qr + 1) * R]),
            "x_red": np.ascontiguousarray(xb[q2 * R:(q2 + 1) * R]),
            "xT_half": xT_half,
            "wq_in": wq, "wk_in": wk, "wv_in": wv,
            "w_half": np.ascontiguousarray(w[par * 512:(par + 1) * 512]),
            "dw_half": np.ascontiguousarray(dw[par * 512:(par + 1) * 512]),
            "bias_in": bias, "dact_in": dact,
            "ident_bf": ident,
            "par_in": np.array([[par]], dtype=np.uint32),
        })
    res = run_bass_kernel_spmd(nc, in_maps, core_ids=list(range(8)))
    y = np.empty((B, S, D), dtype=np.float32)
    for c in range(8):
        b = c // 4
        qr = c % 4
        y[b, qr * R:(qr + 1) * R] = res.results[c]["y_out"]
    return y


# version 10
# speedup vs baseline: 2.3544x; 2.2063x over previous
import numpy as np
import ml_dtypes
import concourse.bass as bass
import concourse.mybir as mybir
from concourse.bass_utils import run_bass_kernel_spmd
from concourse.alu_op_type import AluOpType
from contextlib import ExitStack

F32 = mybir.dt.float32
BF16 = mybir.dt.bfloat16
I32 = mybir.dt.int32
U32 = mybir.dt.uint32
AF = mybir.ActivationFunctionType
bf16 = ml_dtypes.bfloat16

D = 1024
H = 64
NB = D // 32
LN2 = float(np.log(2.0))
EXPB = -20.0
EPS = 1e-5
TINY = 1e-30
MAGIC_C1 = (22 << 23) | 0x400000
EXPMASK = 0x7F800000
ONEBITS = 127 << 23


class Prog:
    ENGINES = ("sync", "tensor", "vector", "scalar", "gpsimd")

    def __init__(self, nc):
        self.nc = nc
        self.ops = {e: [] for e in self.ENGINES}
        self.seq = {e: 0 for e in self.ENGINES}
        self.sems = {}
        self.dcnt = {}
        self.state = {}
        self.water = {}
        self._es = ExitStack()
        self.POOL_N = 12
        self.EPOOL_N = 8
        self.pool_rr2 = {}
        self.epool_rr = {e: 0 for e in self.ENGINES}

    def sem(self, name):
        if name not in self.sems:
            self.sems[name] = self._es.enter_context(self.nc.semaphore(name))
        return self.sems[name]

    def _wait(self, eng, marker):
        if marker is None:
            return None
        sname, val = marker
        key = (eng, sname)
        if self.water.get(key, 0) >= val:
            return None
        self.water[key] = val
        return (self.sems[sname], val)

    def op(self, eng, fn, reads=(), writes=(), dma=None, manual_sem=None):
        waits = []

        def add(m):
            w = self._wait(eng, m)
            if w:
                waits.append(w)

        for b in reads:
            st = self.state.get(b)
            if st and st["w"]:
                add(st["w"])
        for b in writes:
            st = self.state.get(b)
            if st:
                if st["w"]:
                    add(st["w"])
                for r in st["r"]:
                    add(r)
        if dma is not None or manual_sem is not None:
            pk = "sw" if eng == "gpsimd" else "hw"
            j = self.pool_rr2.get(pk, 0) % self.POOL_N
            self.pool_rr2[pk] = self.pool_rr2.get(pk, 0) + 1
            sname = f"dp_{pk}{j}"
            self.sem(sname)
            prev = self.dcnt.get(sname, 0)
            if prev:
                w = self._wait(eng, (sname, prev))
                if w:
                    waits.append(w)
            self.dcnt[sname] = prev + 16
            marker = (sname, prev + 16)
            if manual_sem is not None:
                realfn = fn
                semh = self.sems[sname]
                fn = (lambda realfn=realfn, semh=semh: (lambda e: realfn(e, semh)))()
                inc = None
            else:
                inc = (self.sems[sname], 16)
        else:
            j = self.epool_rr[eng] % self.EPOOL_N
            self.epool_rr[eng] += 1
            sname = f"e_{eng}{j}"
            self.sem(sname)
            prev = self.dcnt.get(sname, 0)
            if prev:
                w = self._wait(eng, (sname, prev))
                if w:
                    waits.append(w)
            self.dcnt[sname] = prev + 1
            marker = (sname, prev + 1)
            inc = (self.sems[sname], 1)
        for b in reads:
            self.state.setdefault(b, {"w": None, "r": []})["r"].append(marker)
        for b in writes:
            self.state[b] = {"w": marker, "r": []}
        self.ops[eng].append((waits, fn, inc))
        return marker

    def raw(self, eng, fn):
        self.ops[eng].append(([], fn, None))

    def emit(self):
        nc = self.nc
        with nc.Block() as block:
            for ename, deco in (("sync", block.sync), ("tensor", block.tensor),
                                ("vector", block.vector), ("scalar", block.scalar),
                                ("gpsimd", block.gpsimd)):
                ops = self.ops[ename]

                def make(ops_list):
                    def body(eng):
                        for waits, fn, inc in ops_list:
                            for sem, val in waits:
                                eng.wait_ge(sem, val)
                            if fn is None:
                                continue
                            inst = fn(eng)
                            if inc is not None:
                                if inst is None:
                                    eng.sem_inc(inc[0], inc[1])
                                else:
                                    inst.then_inc(inc[0], inc[1])
                    return body
                deco(make(ops))


KILL = {}


def build_nc(R=2048, KPG=4, debug_taps=False, stage=9):
    S = 4 * R
    NT = R // 128
    KT = S // 128
    NG = KT // KPG
    assert NG % 2 == 0
    CH = min(512, R)
    NCH = R // CH
    ECH = min(512, R)
    NEC = R // ECH
    CPE = ECH // CH
    COLH = min(2048, R)
    NCOLH = (2 * R) // COLH
    CC_N = COLH // 512 if COLH >= 512 else 1
    CW = min(512, COLH)

    nc = bass.Bass("TRN2", target_bir_lowering=False, debug=False,
                   detect_race_conditions=False)

    x_own = nc.dram_tensor("x_own", [R, D], F32, kind="ExternalInput")
    x_red = nc.dram_tensor("x_red", [R, D], F32, kind="ExternalInput")
    xT_half = nc.dram_tensor("xT_half", [D, 2 * R], F32, kind="ExternalInput")
    wq_in = nc.dram_tensor("wq_in", [D, H], F32, kind="ExternalInput")
    wk_in = nc.dram_tensor("wk_in", [D, H], F32, kind="ExternalInput")
    wv_in = nc.dram_tensor("wv_in", [D, H], F32, kind="ExternalInput")
    w_half = nc.dram_tensor("w_half", [D // 2, D], F32, kind="ExternalInput")
    dw_half = nc.dram_tensor("dw_half", [D // 2, NB], F32, kind="ExternalInput")
    bias_in = nc.dram_tensor("bias_in", [1, D], F32, kind="ExternalInput")
    dact_in = nc.dram_tensor("dact_in", [128, 1], F32, kind="ExternalInput")
    ident_bf = nc.dram_tensor("ident_bf", [128, 128], BF16, kind="ExternalInput")
    par_in = nc.dram_tensor("par_in", [1, 1], U32, kind="ExternalInput")
    y_out = nc.dram_tensor("y_out", [R, D], F32, kind="ExternalOutput")

    xq_dram = nc.dram_tensor("xq_dram", [R, D], BF16)
    mr_dram = nc.dram_tensor("mr_dram", [64, 128], BF16)
    delay_dram = nc.dram_tensor("delay_dram", [128, 8192], F32)
    delay_dram2 = nc.dram_tensor("delay_dram2", [128, 8192], F32)
    hello_flag = nc.dram_tensor("hello_flag", [2, 16], U32, addr_space="Shared")
    kv_sh = nc.dram_tensor("kv_sh", [2 * 128, 2 * R], BF16, addr_space="Shared")
    rr_sh = nc.dram_tensor("rr_sh", [2 * 128, 2 * NT], F32, addr_space="Shared")
    kv_flag = nc.dram_tensor("kv_flag", [2, 16], U32, addr_space="Shared")
    z_sh = nc.dram_tensor("z_sh", [2 * NG, 128, KPG], F32, addr_space="Shared")
    z_flag = nc.dram_tensor("z_flag", [2 * NG, 16], U32, addr_space="Shared")
    wq_sh = nc.dram_tensor("wq_sh", [D, D], BF16, addr_space="Shared")
    wq_loc = nc.dram_tensor("wq_loc", [D, D], BF16)
    wq_flag = nc.dram_tensor("wq_flag", [2, 16], U32, addr_space="Shared")

    sb = lambda n, s, d: nc.alloc_sbuf_tensor(n, s, d)

    def sb_at(name, shape, dtype, host):
        off = nc.lookup_mloc(host).addr
        return nc.alloc_sbuf_tensor_at(name, shape, dtype, offset=off)

    xt = [sb(f"xt{i}", [128, D], F32) for i in range(NT)]
    stat = sb("stat", [128, 12], F32)
    stat2 = sb("stat2", [128, 2], F32)
    m_own = sb("m_own", [128, NT], F32)
    r_own = sb("r_own", [128, NT], F32)
    m_red = sb("m_red", [128, NT], F32)
    r_red = sb("r_red", [128, NT], F32)
    negmr = sb("negmr", [128, NT], F32)
    negm = sb("negm", [128, NT], F32)
    var_t = sb("var_t", [128, NT], F32)
    mr_pack = sb("mr_pack", [128, 64], BF16)
    mrT = sb("mrT", [64, 128], BF16)
    m_row = sb("m_row", [1, 2 * R], BF16)
    rrow_tmp = sb("rrow_tmp", [1, 2 * R], BF16)
    r_rowq = sb("r_rowq", [64, 2 * R], BF16)
    r_cols = sb("r_cols", [128, 2 * KT], F32)
    idb = sb("idb", [128, 128], BF16)
    identf = sb("identf", [64, 64], F32)
    ones_row = sb("ones_row", [1, 128], BF16)
    ones_col = sb("ones_col", [128, 1], BF16)
    wkv_s = sb("wkv_s", [128, 8 * 128], BF16)
    wq_s = sb("wq_s", [128, 8 * 64], BF16)
    cw = sb("cw", [1, 192], BF16)
    xTs = [sb(f"xTs{i}", [128, COLH], BF16) for i in range(2)]
    kvT = sb("kvT", [128, 2 * R], BF16)
    qTo = sb("qTo", [128, R], BF16)
    qTr = sb("qTr", [128, R], BF16)
    kTd = sb("kTd", [128, S], BF16)
    vtok = sb("vtok", [128, KT * H], BF16)
    if 2 * S >= 5 * D + D:
        wt_in = kTd[:, 0:2 * D].bitcast(F32)
        wmg = kTd[:, 2 * D:4 * D].bitcast(F32)
        wq_bf = kTd[:, 4 * D:5 * D]
        WQ_ALIAS = ["kTd_m", "kTd_p"]
    else:
        wt_in = sb("wt_in", [128, D], F32)
        wmg = sb("wmg", [128, D], F32)
        wq_bf = sb("wq_bf", [128, D], BF16)
        WQ_ALIAS = []
    pbuf = [sb(f"pbuf{i}", [128, KPG * R], BF16) for i in range(3)]
    trash = sb("trash", [128, ECH], BF16)
    delay_t = trash[:, 0:ECH].bitcast(F32)
    zacc = sb("zacc", [128, KT * 8], F32)
    zmine = sb("zmine", [128, KT], F32)
    zpart = sb("zpart", [128, KPG], F32)
    zfull = sb("zfull", [128, KPG], F32)
    vzs = sb("vzs", [128, KPG], F32)
    vpp = [sb(f"vpp{i}", [128, KPG * H], BF16) for i in range(2)]
    stok = qTo[:, :].bitcast(F32)[:, 0:NT * H]
    if R >= 2 * D:
        xnt = qTr[:, :].bitcast(F32)[:, 0:D]
    else:
        xnt = sb("xnt", [128, D], F32)
    xqt = [sb(f"xqt{i}", [128, D], F32) for i in range(2)]
    xrt = xqt
    amax = sb("amax", [128, NB], F32)
    qs_rs = sb("qs_rs", [128, NB], F32)
    qs_sc = sb("qs_sc", [128, NB], F32)
    if COLH >= 2 * D:
        mg_t = xTs[1][:, :].bitcast(F32)
    else:
        mg_t = sb("mg_t", [128, D], F32)
    wamax = sb("wamax", [128, NB], F32)
    wrs = sb("wrs", [128, NB], F32)
    wsc = sb("wsc", [128, NB], F32)
    wfac = sb("wfac", [128, 4 * NB], F32)
    wrfac = sb("wrfac", [128, 4 * NB], F32)
    dwt = sb("dwt", [128, 4 * NB], F32)
    cact = sb("cact", [128, 4], F32)
    cvec = sb("cvec", [128, 2], F32)
    bias_bf = sb("bias_bf", [1, D], BF16)
    if COLH >= 2 * D:
        gt = xTs[0][:, :].bitcast(F32)
        ytt = xTs[1][:, :].bitcast(F32)
    else:
        gt = sb("gt", [128, D], F32)
        ytt = sb("ytt", [128, D], F32)
    flagt = sb("flagt", [1, 4], U32)
    expb_t = sb("expb_t", [128, 1], F32)
    onep = sb("onep", [128, 1], F32)
    lnb_t = sb("lnb_t", [128, 2], F32)
    onesu = sb("onesu", [1, 16], U32)
    part = sb("part", [1, 1], U32)

    sfeat = kvT[0:64, :].bitcast(F32)
    vstg = pbuf[0][0:64, 0:2 * R]

    if 2 * KPG * R >= 8 * R and KPG * R >= 8 * D:
        def xqT_strip(d):
            b = d // KPG
            o = (d % KPG) * R
            return pbuf[b][:, o:o + R]

        def wqT_strip(d):
            return pbuf[2][:, d * D:(d + 1) * D]
        XQT_KEYS = [("pbuf", 0), ("pbuf", 1)]
        WQT_KEYS = [("pbuf", 2)]
    else:
        xqTs_t = sb("xqTs", [128, 8 * R], BF16)
        wqTs_t = sb("wqTs", [128, 8 * D], BF16)

        def xqT_strip(d):
            return xqTs_t[:, d * R:(d + 1) * R]

        def wqT_strip(d):
            return wqTs_t[:, d * D:(d + 1) * D]
        XQT_KEYS = ["xqTs_t"]
        WQT_KEYS = ["wqTs_t"]

    psA = nc.alloc_psum_tensor("psA", [128, 1024], F32)
    psB = nc.alloc_psum_tensor("psB", [128, 1024], F32)
    out_ps = nc.alloc_psum_tensor("out_ps", [128, R], F32)
    PSA = lambda h: ("psA", h)
    PSB = lambda h: ("psB", h)

    P = Prog(nc)
    preg = nc.alloc_register(mybir.EngineType.Pool, "preg")
    spreg = nc.alloc_register(mybir.EngineType.Pool, "spreg")
    RH = bass.RegisterHandles
    def bailout():
        marks = []
        for u in range(NT):
            m = P.op("sync", lambda e, u=u: e.dma_start(out=y_out[u * 128:(u + 1) * 128, :],
                                                        in_=xt[u][:]),
                     reads=[("xt", u)], dma="yo")
            marks.append(m)

        def fw(e):
            for sname, val in marks:
                e.wait_ge(P.sems[sname], val)
            return None
        P.raw("sync", fw)
        P.emit()


    def cond2(e, sem, fn0, fn1):
        with nc.If_cmp(RH((preg,)), 0, "IS_EQ"):
            fn0(e).then_inc(sem, 16)
        nc.end_ifs()
        with nc.If_cmp(RH((preg,)), 0, "IS_NE"):
            fn1(e).then_inc(sem, 16)
        nc.end_ifs()
        return None

    spreg2 = nc.alloc_register(mybir.EngineType.Pool, "spreg2")
    syreg = nc.alloc_register(mybir.EngineType.Activation, "syreg")

    def spin(tag, flag_row0, flag_row1, K=4, FLAT=12, CWF=2048, TAIL=(2, 4, 8, 16, 32, 64),
             CW=8192):
        bar = P.sem("bar_" + tag)
        fp = P.sem("fp_" + tag)
        dly = P.sem("dly")
        scheds = [0] * K + [(1, CWF)] * FLAT + [(n, CW) for n in TAIL]

        def f(e):
            e.reg_alu(spreg, spreg, spreg, AluOpType.bitwise_xor)
            for k, sch in enumerate(scheds):
                with nc.If_cmp(bass.RegisterHandles((spreg,)), 2, "IS_LT"):
                    if sch:
                        nd, cw = sch
                        for _ in range(nd):
                            e.dma_start(out=delay_dram2[:, 0:cw],
                                        in_=delay_dram[:, 0:cw]).then_inc(dly, 16)
                    e.dma_start(out=flagt[0:1, 0:1], in_=flag_row0).then_inc(fp, 16)
                    e.dma_start(out=flagt[0:1, 1:2], in_=flag_row1).then_inc(fp, 16)
                    e.wait_ge(fp, 32 * (k + 1))
                    e.reg_load(spreg, flagt[0:1, 0:1])
                    e.reg_load(spreg2, flagt[0:1, 1:2])
                    e.reg_alu(spreg, spreg, spreg2, AluOpType.add)
                nc.end_ifs()
                with nc.If_cmp(bass.RegisterHandles((spreg,)), 2, "IS_GE"):
                    e.sem_inc(bar, 1)
                nc.end_ifs()
            e.wait_ge(bar, 1)
            return None
        P.op("gpsimd", f, reads=["part_reg", "flagt", "delay_dram"],
             writes=[("bar", tag), "flagt"])
        return ("bar_" + tag, 1)

    P.op("sync", lambda e: e.dma_start(out=idb[:], in_=ident_bf[:, :]), writes=["idb"], dma="cst")
    P.op("gpsimd", lambda e: e.dma_start(out=bias_bf[:], in_=bias_in[:, :]), writes=["bias_bf"], dma="cst2")
    P.op("sync", lambda e: e.dma_start(out=cact[:, 0:1], in_=dact_in[:, :]), writes=["cact0"], dma="cst")
    P.op("sync", lambda e: e.dma_start(out=part[:], in_=par_in[:, :]), writes=["part"], dma="cst")
    if stage <= -2:
        bailout()
        return nc

    P.op("gpsimd", lambda e: e.memset(onesu[:], 1), writes=["onesu"])
    P.op("gpsimd", lambda e: e.memset(ones_row[:], 1.0), writes=["ones_row"])
    P.op("gpsimd", lambda e: e.memset(ones_col[:], 1.0), writes=["ones_col"])
    P.op("gpsimd", lambda e: e.memset(expb_t[:], EXPB), writes=["expb"])
    P.op("gpsimd", lambda e: e.memset(onep[:], 1.0), writes=["onep"])
    if KILL.get("delayinit", False) is False:
        P.op("gpsimd", lambda e: e.memset(trash[:].bitcast(F32), 0.0), writes=["trash"])
        DCH_W = ECH // 2
        for ci in range(8192 // DCH_W):
            P.op("gpsimd", lambda e, ci=ci: e.dma_start(
                out=delay_dram[:, ci * DCH_W:(ci + 1) * DCH_W], in_=delay_t),
                reads=["trash"], writes=["delay_dram"], dma="gc")
    P.op("gpsimd", lambda e: e.memset(lnb_t[:, 0:1], -2.0 * LN2), writes=["lnb0"])
    P.op("gpsimd", lambda e: e.memset(lnb_t[:, 1:2], 1.0 * LN2), writes=["lnb1"])

    def loadreg(e):
        e.reg_load(preg, part[0:1, 0:1])
        e.reg_load(spreg, part[0:1, 0:1])
        e.reg_load(spreg2, part[0:1, 0:1])
        return None
    P.op("gpsimd", loadreg, reads=["part"], writes=["part_reg"])

    def loadreg_sy(e):
        e.reg_load(syreg, part[0:1, 0:1])
        return None
    P.op("scalar", loadreg_sy, reads=["part"], writes=["part_reg_sy"])

    def cond2s(e, sem, fn0, fn1):
        with nc.If_cmp(RH((syreg,)), 0, "IS_EQ"):
            fn0(e).then_inc(sem, 16)
        nc.end_ifs()
        with nc.If_cmp(RH((syreg,)), 0, "IS_NE"):
            fn1(e).then_inc(sem, 16)
        nc.end_ifs()
        return None

    P.op("gpsimd", lambda e, sem: cond2(
        e, sem, lambda e2: e2.dma_start(out=hello_flag[0:1, :], in_=onesu[:]),
        lambda e2: e2.dma_start(out=hello_flag[1:2, :], in_=onesu[:])),
        reads=["onesu", "part_reg"], writes=["hello_w"], manual_sem=True)
    spin("hello", hello_flag[0:1, 0:1], hello_flag[1:2, 0:1], K=3, FLAT=4,
         TAIL=(2, 4, 8, 16, 32, 64, 64, 64, 64, 64, 64))

    P.op("gpsimd", lambda e: e.dma_start(
        out=wkv_s[:].rearrange("p (t m) -> p t m", t=8)[:, :, 0:64],
        in_=wk_in[:, :].rearrange("(t p) h -> p t h", p=128)), writes=["wkv_k"], dma="gc")
    P.op("gpsimd", lambda e: e.dma_start(
        out=wkv_s[:].rearrange("p (t m) -> p t m", t=8)[:, :, 64:128],
        in_=wv_in[:, :].rearrange("(t p) h -> p t h", p=128)), writes=["wkv_v"], dma="gc")
    P.op("gpsimd", lambda e: e.dma_start(
        out=wq_s[:].rearrange("p (t h) -> p t h", t=8),
        in_=wq_in[:, :].rearrange("(t p) h -> p t h", p=128)), writes=["wq_s"], dma="gc")
    P.op("gpsimd", lambda e: e.dma_start(
        out=dwt[:].rearrange("p (t b) -> p t b", t=4),
        in_=dw_half[:, :].rearrange("(t p) b -> p t b", p=128)), writes=["dwt"], dma="gc")

    if stage <= -1:
        bailout()
        return nc

    P.op("scalar", lambda e: e.activation(cact[:, 2:3], cact[:, 0:1], AF.Exp, scale=8.0),
         reads=["cact0"], writes=["cact2"])
    P.op("vector", lambda e: e.tensor_scalar(cact[:, 2:3], cact[:, 2:3], 1.0, None,
                                             AluOpType.add),
         reads=["cact2"], writes=["cact2"])
    P.op("vector", lambda e: e.reciprocal(cact[:, 3:4], cact[:, 2:3]),
         reads=["cact2"], writes=["cact3"])
    P.op("vector", lambda e: e.tensor_scalar(cact[:, 1:2], cact[:, 3:4], -2.0, 1.0,
                                             AluOpType.mult, AluOpType.add),
         reads=["cact3"], writes=["cact1"])
    P.op("scalar", lambda e: e.activation(cvec[:, 0:1], cact[:, 1:2], AF.Exp,
                                          scale=0.5 * LN2, bias=lnb_t[:, 0:1]),
         reads=["cact1", "lnb0"], writes=["cvec"])
    P.op("scalar", lambda e: e.activation(cvec[:, 1:2], cact[:, 1:2], AF.Exp,
                                          scale=-0.5 * LN2, bias=lnb_t[:, 1:2]),
         reads=["cact1", "lnb1", "cvec"], writes=["cvec"])

    def cw_mms(e):
        last = None
        for t in range(8):
            last = e.matmul(psB[0:1, 512:640], ones_col[:, :], wkv_s[:, t * 128:(t + 1) * 128],
                            start=(t == 0), stop=(t == 7))
        for t in range(8):
            last = e.matmul(psB[0:1, 640:704], ones_col[:, :], wq_s[:, t * 64:(t + 1) * 64],
                            start=(t == 0), stop=(t == 7))
        return last
    P.op("tensor", cw_mms, reads=["wkv_k", "wkv_v", "wq_s", "ones_col"], writes=[PSB(1)])
    P.op("vector", lambda e: e.tensor_copy(cw[0:1, :], psB[0:1, 512:704]),
         reads=[PSB(1)], writes=["cw"])

    for i in range(NT):
        P.op("sync", lambda e, i=i: e.dma_start(out=xt[i][:], in_=x_own[i * 128:(i + 1) * 128, :]),
             writes=[("xt", i)], dma="xo")
    for i in range(NT):
        P.op("vector", lambda e, i=i: e.bn_stats(stat[:, 0:6], xt[i][:, 0:512]),
             reads=[("xt", i)], writes=["stat6a"])
        P.op("vector", lambda e, i=i: e.bn_stats(stat[:, 6:12], xt[i][:, 512:1024]),
             reads=[("xt", i)], writes=["stat6b"])
        P.op("vector", lambda e: e.bn_aggr(stat2[:, 0:2], stat[:, 0:12]),
             reads=["stat6a", "stat6b"], writes=["stat"])
        P.op("vector", lambda e, i=i: e.tensor_copy(m_own[:, i:i + 1], stat2[:, 0:1]),
             reads=["stat"], writes=["m_own"])
        P.op("vector", lambda e, i=i: e.tensor_scalar(var_t[:, i:i + 1], stat2[:, 1:2], EPS, None,
                                                      AluOpType.add),
             reads=["stat"], writes=["var"])
    P.op("scalar", lambda e: e.activation(var_t[:], var_t[:], AF.Ln), reads=["var"], writes=["var"])
    P.op("scalar", lambda e: e.activation(r_own[:], var_t[:], AF.Exp, scale=-0.5),
         reads=["var"], writes=["r_own"])
    for i in range(NT):
        P.op("scalar", lambda e, i=i: e.dma_start(out=xrt[i % 2][:], in_=x_red[i * 128:(i + 1) * 128, :]),
             writes=[("xqt", i % 2)], dma="xr")
        P.op("vector", lambda e, i=i: e.bn_stats(stat[:, 0:6], xrt[i % 2][:, 0:512]),
             reads=[("xqt", i % 2)], writes=["stat6a"])
        P.op("vector", lambda e, i=i: e.bn_stats(stat[:, 6:12], xrt[i % 2][:, 512:1024]),
             reads=[("xqt", i % 2)], writes=["stat6b"])
        P.op("vector", lambda e: e.bn_aggr(stat2[:, 0:2], stat[:, 0:12]),
             reads=["stat6a", "stat6b"], writes=["stat"])
        P.op("vector", lambda e, i=i: e.tensor_copy(m_red[:, i:i + 1], stat2[:, 0:1]),
             reads=["stat"], writes=["m_red"])
        P.op("vector", lambda e, i=i: e.tensor_scalar(var_t[:, i:i + 1], stat2[:, 1:2], EPS, None,
                                                      AluOpType.add),
             reads=["stat"], writes=["varr"])
    P.op("scalar", lambda e: e.activation(var_t[:], var_t[:], AF.Ln), reads=["varr"], writes=["varr"])
    P.op("scalar", lambda e: e.activation(r_red[:], var_t[:], AF.Exp, scale=-0.5),
         reads=["varr"], writes=["r_red"])
    P.op("vector", lambda e: e.tensor_mul(negmr[:], m_own[:], r_own[:]),
         reads=["m_own", "r_own"], writes=["negmr"])
    P.op("vector", lambda e: e.tensor_scalar(negmr[:], negmr[:], -1.0, None, AluOpType.mult),
         reads=["negmr"], writes=["negmr"])
    P.op("vector", lambda e: e.tensor_scalar(negm[:], m_own[:], -1.0, None, AluOpType.mult),
         reads=["m_own"], writes=["negm"])

    P.op("vector", lambda e: e.memset(mr_pack[:], 0.0), writes=["mrp"])
    P.op("vector", lambda e: e.tensor_scalar(mr_pack[:, 0:NT], m_own[:], -1.0, None, AluOpType.mult),
         reads=["m_own", "mrp"], writes=["mrp"])
    P.op("vector", lambda e: e.tensor_scalar(mr_pack[:, 16:16 + NT], m_red[:], -1.0, None,
                                             AluOpType.mult),
         reads=["m_red", "mrp"], writes=["mrp"])
    P.op("vector", lambda e: e.tensor_copy(mr_pack[:, 32:32 + NT], r_own[:]),
         reads=["r_own", "mrp"], writes=["mrp"])
    P.op("vector", lambda e: e.tensor_copy(mr_pack[:, 48:48 + NT], r_red[:]),
         reads=["r_red", "mrp"], writes=["mrp"])
    P.op("tensor", lambda e: e.transpose(psB[0:64, 0:64].bitcast(BF16), mr_pack[:, 0:64], idb[:, :]),
         reads=["mrp", "idb"], writes=[PSB(0)])
    P.op("vector", lambda e: e.tensor_copy(mrT[:, :], psB[0:64, 0:64].bitcast(BF16)),
         reads=[PSB(0)], writes=["mrT"])
    P.op("gpsimd", lambda e: e.dma_start(out=mr_dram[:, :], in_=mrT[:, :]),
         reads=["mrT"], writes=["mr_dram"], dma="mr")
    P.op("sync", lambda e: e.dma_start(
        out=m_row[:, :].rearrange("o (h t p) -> o h t p", h=2, t=NT),
        in_=mr_dram[0:32, :].rearrange("(o h t) p -> o h t p", o=1, h=2)[:, :, 0:NT]),
        reads=["mr_dram"], writes=["m_row"], dma="mrr")
    P.op("sync", lambda e: e.dma_start(
        out=rrow_tmp[:, :].rearrange("o (h t p) -> o h t p", h=2, t=NT),
        in_=mr_dram[32:64, :].rearrange("(o h t) p -> o h t p", o=1, h=2)[:, :, 0:NT]),
        reads=["mr_dram"], writes=["rrow_tmp"], dma="mrr")
    for ci in range(2 * R // 512):
        half = ci % 2
        P.op("tensor", lambda e, ci=ci, half=half: e.matmul(
            psA[0:64, half * 512:half * 512 + 512], ones_row[0:1, 0:64],
            rrow_tmp[0:1, ci * 512:(ci + 1) * 512], start=True, stop=True),
            reads=["rrow_tmp", "ones_row"], writes=[PSA(half)])
        P.op("vector", lambda e, ci=ci, half=half: e.tensor_copy(
            r_rowq[:, ci * 512:(ci + 1) * 512], psA[0:64, half * 512:half * 512 + 512]),
            reads=[PSA(half)], writes=["r_rowq"])
    P.op("vector", lambda e: e.tensor_copy(r_cols[:, 0:NT], r_own[:]),
         reads=["r_own"], writes=["r_cols_m"])
    P.op("vector", lambda e: e.tensor_copy(r_cols[:, NT:2 * NT], r_red[:]),
         reads=["r_red", "r_cols_m"], writes=["r_cols_m"])

    if stage <= 0:
        bailout()
        return nc

    for ch in range(NCOLH):
        buf = ch % 2
        c0 = ch * COLH
        for d in range(8):
            P.op("gpsimd", lambda e, d=d, c0=c0, buf=buf: e.dma_start(
                out=xTs[buf][:], in_=xT_half[d * 128:(d + 1) * 128, c0:c0 + COLH]),
                writes=[("xTs", buf)], dma="xT")

            def mmkv(e, d=d, buf=buf):
                last = None
                for cc in range(CC_N):
                    last = e.matmul(out_ps[:, cc * 512:cc * 512 + CW],
                                    wkv_s[:, d * 128:(d + 1) * 128],
                                    xTs[buf][:, cc * 512:cc * 512 + CW],
                                    start=(d == 0), stop=False)
                for cc in range(CC_N):
                    tgt, off = (psA, (cc % 2) * 512) if cc < 2 else (psB, (cc % 2) * 512)
                    last = e.matmul(tgt[0:64, off:off + CW],
                                    wq_s[:, d * 64:(d + 1) * 64],
                                    xTs[buf][:, cc * 512:cc * 512 + CW],
                                    start=(d == 0), stop=False)
                return last
            P.op("tensor", mmkv, reads=[("xTs", buf), "wkv_k", "wkv_v", "wq_s"],
                 writes=[("outps",), PSA(0), PSA(1), PSB(0), PSB(1)])

        def fix(e, c0=c0):
            last = None
            for cc in range(CC_N):
                last = e.matmul(out_ps[:, cc * 512:cc * 512 + CW], cw[0:1, 0:128],
                                m_row[0:1, c0 + cc * 512:c0 + cc * 512 + CW],
                                start=False, stop=True)
                tgt, off = (psA, (cc % 2) * 512) if cc < 2 else (psB, (cc % 2) * 512)
                last = e.matmul(tgt[0:64, off:off + CW], cw[0:1, 128:192],
                                m_row[0:1, c0 + cc * 512:c0 + cc * 512 + CW],
                                start=False, stop=True)
            return last
        P.op("tensor", fix, reads=["cw", "m_row", ("outps",), PSA(0), PSA(1), PSB(0), PSB(1)],
             writes=[("outps",), PSA(0), PSA(1), PSB(0), PSB(1)])
        for cc in range((COLH + 511) // 512):
            wdt = min(512, COLH - cc * 512)
            P.op("vector", lambda e, c0=c0, cc=cc, wdt=wdt: e.tensor_copy(
                kvT[:, c0 + cc * 512:c0 + cc * 512 + wdt], out_ps[:, cc * 512:cc * 512 + wdt]),
                reads=[("outps",)], writes=["kvT"])

        for cc in range(CC_N):
            tgt, off = (psA, (cc % 2) * 512) if cc < 2 else (psB, (cc % 2) * 512)
            gc0 = c0 + cc * 512
            dst = qTo if gc0 < R else qTr
            do = gc0 % R
            pk = PSA(cc % 2) if cc < 2 else PSB(cc % 2)
            P.op("vector", lambda e, tgt=tgt, off=off, dst=dst, do=do, gc0=gc0:
                 e.scalar_tensor_tensor(dst[0:64, do:do + CW], tgt[0:64, off:off + CW],
                                        1.0, r_rowq[0:64, gc0:gc0 + CW],
                                        AluOpType.mult, AluOpType.mult),
                 reads=[pk, "r_rowq"], writes=["qT0"])
    P.op("vector", lambda e: e.tensor_copy(qTo[64:128, :], qTo[0:64, :]),
         reads=["qT0"], writes=["qTo"])
    P.op("vector", lambda e: e.tensor_copy(qTr[64:128, :], qTr[0:64, :]),
         reads=["qT0", "qTo"], writes=["qTr"])

    if stage <= -1:
        pass
    if stage == 5:
        bailout()
        return nc

    P.op("gpsimd", lambda e, sem: cond2(
        e, sem, lambda e2: e2.dma_start(out=kv_sh[0:128, :], in_=kvT[:]),
        lambda e2: e2.dma_start(out=kv_sh[128:256, :], in_=kvT[:])),
        reads=["kvT", "part_reg"], writes=["kv_sh_w"], manual_sem=True)
    P.op("gpsimd", lambda e, sem: cond2(
        e, sem, lambda e2: e2.dma_start(out=rr_sh[0:128, :], in_=r_cols[:, 0:2 * NT]),
        lambda e2: e2.dma_start(out=rr_sh[128:256, :], in_=r_cols[:, 0:2 * NT])),
        reads=["r_cols_m", "part_reg"], writes=["rr_sh_w"], manual_sem=True)
    P.op("gpsimd", lambda e, sem: cond2(
        e, sem, lambda e2: e2.dma_start(out=kv_flag[0:1, :], in_=onesu[:]),
        lambda e2: e2.dma_start(out=kv_flag[1:2, :], in_=onesu[:])),
        reads=["onesu", "part_reg", "kv_sh_w", "rr_sh_w"], writes=["kv_flag_w"],
        manual_sem=True)
    kv_bar = spin("kv", kv_flag[0:1, 0:1], kv_flag[1:2, 0:1])
    P.op("gpsimd", lambda e, sem: cond2(
        e, sem, lambda e2: e2.dma_start(out=kTd[0:64, 2 * R:4 * R], in_=kv_sh[128:192, :]),
        lambda e2: e2.dma_start(out=kTd[0:64, 2 * R:4 * R], in_=kv_sh[0:64, :])),
        reads=[("bar", "kv"), "part_reg"], writes=["kTd_p"], manual_sem=True)
    P.op("gpsimd", lambda e, sem: cond2(
        e, sem, lambda e2: e2.dma_start(out=kTd[64:128, 2 * R:4 * R], in_=kv_sh[128:192, :]),
        lambda e2: e2.dma_start(out=kTd[64:128, 2 * R:4 * R], in_=kv_sh[0:64, :])),
        reads=[("bar", "kv"), "part_reg", "kTd_p"], writes=["kTd_p"], manual_sem=True)
    P.op("gpsimd", lambda e, sem: cond2(
        e, sem, lambda e2: e2.dma_start(out=vstg, in_=kv_sh[192:256, :]),
        lambda e2: e2.dma_start(out=vstg, in_=kv_sh[64:128, :])),
        reads=[("bar", "kv"), "part_reg"], writes=[("pbuf", 0)], manual_sem=True)
    P.op("gpsimd", lambda e, sem: cond2(
        e, sem, lambda e2: e2.dma_start(out=r_cols[:, 2 * NT:4 * NT], in_=rr_sh[128:256, :]),
        lambda e2: e2.dma_start(out=r_cols[:, 2 * NT:4 * NT], in_=rr_sh[0:128, :])),
        reads=[("bar", "kv"), "part_reg", "r_cols_m"], writes=["r_cols"], manual_sem=True)
    P.op("vector", lambda e: e.tensor_copy(kTd[0:64, 0:2 * R], kvT[0:64, :]),
         reads=["kvT"], writes=["kTd_m"])
    P.op("vector", lambda e: e.tensor_copy(kTd[64:128, 0:2 * R], kvT[0:64, :]),
         reads=["kvT", "kTd_m"], writes=["kTd_m"])
    for t in range(KT):
        if t < KT // 2:
            src = kvT[64:128, t * 128:(t + 1) * 128]
            rk = ["kvT"]
            ident = idb[64:128, 64:128]
        else:
            src = vstg[:, (t - KT // 2) * 128:(t - KT // 2 + 1) * 128]
            rk = [("pbuf", 0)]
            ident = idb[0:64, 0:64]
        half = t % 2
        P.op("tensor", lambda e, src=src, half=half, ident=ident: e.transpose(
            psA[:, half * 512:half * 512 + 32].bitcast(BF16), src, ident),
            reads=rk + ["idb"], writes=[PSA(half)])
        P.op("vector", lambda e, t=t, half=half: e.tensor_copy(
            vtok[:, t * H:(t + 1) * H], psA[:, half * 512:half * 512 + 32].bitcast(BF16)),
            reads=[PSA(half)], writes=["vtok"])

    if stage <= 1:
        bailout()
        return nc

    P.op("vector", lambda e: e.memset(zacc[:], 0.0), writes=[("zacc", k) for k in range(KT * 8)])
    gorder = []
    for i in range(NG // 2):
        gorder += [i, i + NG // 2]

    def strip(tile, red, pslot, pb, pcol):
        ps_ = psA if pslot == 0 else psB
        pkey = PSA if pslot == 0 else PSB
        rg = 0 if pslot == 0 else 64
        q = qTr if red else qTo
        qk = "qTr" if red else "qTo"
        kslice = kTd[rg:rg + 64, tile * 128:(tile + 1) * 128]
        for ec in range(NEC):
            half = ec % 2
            def mms(e, ec=ec, kslice=kslice, ps_=ps_, rg=rg, q=q, half=half):
                last = e.ldweights(kslice, tile_position=(rg, 0))
                c = ec * ECH
                last = e.matmul(ps_[:, half * 512:half * 512 + ECH], kslice,
                                q[rg:rg + 64, c:c + ECH],
                                start=True, stop=True, tile_position=(rg, 0))
                return last
            P.op("tensor", mms, reads=["kTd_m", "kTd_p", qk], writes=[pkey(half)])
            acc_i = tile * 8 + (4 if red else 0) + ec
            if red:
                outap = trash[:, 0:ECH]
                wk = ["trash"]
            else:
                outap = pbuf[pb][:, pcol * R + ec * ECH: pcol * R + (ec + 1) * ECH]
                wk = [("pbuf", pb)]
            P.op("scalar", lambda e, outap=outap, ps_=ps_, acc_i=acc_i, tile=tile, half=half:
                 e.activation(outap, ps_[:, half * 512:half * 512 + ECH], AF.Exp,
                              bias=expb_t[:], scale=r_cols[:, tile:tile + 1],
                              accum_out=zacc[:, acc_i:acc_i + 1]),
                 reads=[pkey(half), "r_cols", "expb"],
                 writes=wk + [("zacc", acc_i)])

    def emit_group_strips(si, g):
        pb = si % 3
        for j in range(KPG):
            strip(g * KPG + j, False, j % 2, pb, j)
        for j in range(KPG):
            strip(g * KPG + j, True, j % 2, pb, j)

        def zsum(e, g=g):
            return e.reduce_sum(zmine[:, g * KPG:(g + 1) * KPG],
                                zacc[:, g * KPG * 8:(g + 1) * KPG * 8]
                                .rearrange("p (t f) -> p t f", f=8),
                                axis=mybir.AxisListType.X)
        P.op("vector", zsum,
             reads=[("zacc", g * KPG * 8 + k) for k in range(KPG * 8)], writes=[("zmine", g)])
        P.op("gpsimd", lambda e, sem, g=g: cond2(
            e, sem, lambda e2: e2.dma_start(out=z_sh[g:g + 1, :, :].rearrange("o p k -> p (o k)"),
                                       in_=zmine[:, g * KPG:(g + 1) * KPG]),
            lambda e2: e2.dma_start(out=z_sh[NG + g:NG + g + 1, :, :].rearrange("o p k -> p (o k)"),
                                    in_=zmine[:, g * KPG:(g + 1) * KPG])),
            reads=[("zmine", g), "part_reg"], writes=[("z_sh", g)], manual_sem=True)
        P.op("gpsimd", lambda e, sem, g=g: cond2(
            e, sem, lambda e2: e2.dma_start(out=z_flag[g:g + 1, :], in_=onesu[:]),
            lambda e2: e2.dma_start(out=z_flag[NG + g:NG + g + 1, :], in_=onesu[:])),
            reads=["onesu", "part_reg", ("z_sh", g)], writes=[("z_flag", g)], manual_sem=True)

    def emit_group_pv(si, g, is_first, is_last):
        pb = si % 3
        gp = (g + NG // 2) % NG
        spin(f"z{g}", z_flag[gp:gp + 1, 0:1], z_flag[NG + gp:NG + gp + 1, 0:1])
        P.op("gpsimd", lambda e, sem, gp=gp: cond2(
            e, sem, lambda e2: e2.dma_start(out=zpart[:, :],
                                       in_=z_sh[NG + gp:NG + gp + 1, :, :].rearrange("o p k -> p (o k)")),
            lambda e2: e2.dma_start(out=zpart[:, :],
                                    in_=z_sh[gp:gp + 1, :, :].rearrange("o p k -> p (o k)"))), reads=[("bar", f"z{g}"), "part_reg"], writes=["zpart"], manual_sem=True)
        P.op("vector", lambda e, g=g: e.tensor_add(zfull[:, :], zmine[:, g * KPG:(g + 1) * KPG],
                                                   zpart[:, :]),
             reads=[("zmine", g), "zpart"], writes=["zfull"])
        P.op("vector", lambda e: e.reciprocal(zfull[:], zfull[:]), reads=["zfull"], writes=["zfull"])
        P.op("vector", lambda e, g=g: e.tensor_mul(vzs[:, :], zfull[:, :],
                                                   r_cols[:, g * KPG:(g + 1) * KPG]),
             reads=["zfull", "r_cols"], writes=["vzs"])
        vslot = si % 2

        for j in range(KPG):
            t = g * KPG + j
            P.op("vector", lambda e, j=j, t=t, vslot=vslot:
                 e.tensor_scalar(vpp[vslot][:, j * H:(j + 1) * H], vtok[:, t * H:(t + 1) * H],
                                 vzs[:, j:j + 1], None, AluOpType.mult),
                 reads=["vtok", "vzs"], writes=[("vpp", vslot)])

        def pv(e, pb=pb, vslot=vslot, is_first=is_first, is_last=is_last):
            last = None
            for j in range(0, KPG, 2):
                e.ldweights(vpp[vslot][:, j * H:(j + 1) * H], tile_position=(0, 0))
                e.ldweights(vpp[vslot][:, (j + 1) * H:(j + 2) * H], tile_position=(0, 64))
                for cc in range(NCH):
                    last = e.matmul(out_ps[0:64, cc * CH:(cc + 1) * CH],
                                    vpp[vslot][:, j * H:(j + 1) * H],
                                    pbuf[pb][:, j * R + cc * CH:j * R + (cc + 1) * CH],
                                    start=(is_first and j == 0),
                                    stop=(is_last and j == KPG - 2),
                                    tile_position=(0, 0))
                    last = e.matmul(out_ps[64:128, cc * CH:(cc + 1) * CH],
                                    vpp[vslot][:, (j + 1) * H:(j + 2) * H],
                                    pbuf[pb][:, (j + 1) * R + cc * CH:(j + 1) * R + (cc + 1) * CH],
                                    start=(is_first and j == 0),
                                    stop=(is_last and j == KPG - 2),
                                    tile_position=(0, 64), skip_group_check=True)
            return last
        P.op("tensor", pv, reads=[("vpp", vslot), ("pbuf", pb)],
             writes=[("outps",), ("pbuf_pv", pb)])

    for si, g in enumerate(gorder):
        emit_group_strips(si, g)
        if si >= 1:
            emit_group_pv(si - 1, gorder[si - 1], si - 1 == 0, si - 1 == len(gorder) - 1)
    emit_group_pv(len(gorder) - 1, gorder[-1], len(gorder) - 1 == 0, True)

    for cc in range(R // 512 if R >= 512 else 1):
        wdt = min(512, R)
        P.op("vector", lambda e, cc=cc, wdt=wdt: e.tensor_copy(
            sfeat[:, cc * 512:cc * 512 + wdt], out_ps[0:64, cc * 512:cc * 512 + wdt]),
            reads=[("outps",)], writes=["sfeat", "kvT"])
        P.op("vector", lambda e, cc=cc, wdt=wdt: e.scalar_tensor_tensor(
            sfeat[:, cc * 512:cc * 512 + wdt], out_ps[64:128, cc * 512:cc * 512 + wdt], 1.0,
            sfeat[:, cc * 512:cc * 512 + wdt], AluOpType.mult, AluOpType.add),
            reads=[("outps",), "sfeat"], writes=["sfeat"])
    P.op("vector", lambda e: e.tensor_copy(identf[:, :], idb[0:64, 0:64]),
         reads=["idb"], writes=["identf"])
    for u in range(NT):
        half = u % 2
        P.op("tensor", lambda e, u=u, half=half: e.transpose(
            psA[:, half * 512:half * 512 + 64], sfeat[:, u * 128:(u + 1) * 128], identf[:, :]),
            reads=["sfeat", "identf"], writes=[PSA(half)])
        P.op("vector", lambda e, u=u, half=half: e.tensor_copy(
            stok[:, u * H:(u + 1) * H], psA[:, half * 512:half * 512 + 64]),
            reads=[PSA(half)], writes=["stok", "qTo"])

    if stage <= 2:
        bailout()
        return nc

    def quant_chain(eng, src_f, dst_f, am_t, rs_t, sc_t, xs_f, mg_f, n, pref, extra_reads,
                    fac_mul=None, fac_rs=None, dst_keys=()):
        nb = n // 32
        P.op("vector", lambda e: e.reduce_max(am_t[:, 0:nb],
                                              src_f().rearrange("p (b e) -> p b e", e=32),
                                              axis=mybir.AxisListType.X,
                                              apply_absolute_value=True),
             reads=extra_reads, writes=[pref + "_amax"])
        P.op(eng, lambda e: e.tensor_scalar(am_t[:, 0:nb], am_t[:, 0:nb], TINY, None,
                                            AluOpType.max),
             reads=[pref + "_amax"], writes=[pref + "_amax"])
        P.op(eng, lambda e: e.tensor_scalar(sc_t[:, 0:nb].bitcast(I32),
                                            am_t[:, 0:nb].bitcast(I32), EXPMASK, None,
                                            AluOpType.bitwise_and),
             reads=[pref + "_amax"], writes=[pref + "_msk"])
        P.op(eng, lambda e: e.tensor_scalar(rs_t[:, 0:nb].bitcast(I32),
                                            sc_t[:, 0:nb].bitcast(I32), EXPMASK, None,
                                            AluOpType.bitwise_xor),
             reads=[pref + "_msk"], writes=[pref + "_rsp0"])
        if fac_mul is not None:
            P.op(eng, lambda e: e.tensor_mul(sc_t[:, 0:nb], sc_t[:, 0:nb], fac_mul()),
                 reads=[pref + "_msk", pref + "_fac"], writes=[pref + "_scp"])
            P.op(eng, lambda e: e.tensor_mul(rs_t[:, 0:nb], rs_t[:, 0:nb], fac_rs()),
                 reads=[pref + "_rsp0", pref + "_fac"], writes=[pref + "_rsp"])
        else:
            P.op(eng, lambda e: e.tensor_scalar(sc_t[:, 0:nb], sc_t[:, 0:nb],
                                                cvec[:, 0:1], None, AluOpType.mult),
                 reads=[pref + "_msk", "cvec"], writes=[pref + "_scp"])
            P.op(eng, lambda e: e.tensor_scalar(rs_t[:, 0:nb], rs_t[:, 0:nb],
                                                cvec[:, 1:2], None, AluOpType.mult),
                 reads=[pref + "_rsp0", "cvec"], writes=[pref + "_rsp"])
        rsv = lambda: rs_t[:, 0:nb].rearrange("p (b o) -> p b o", o=1).broadcast_to((128, nb, 32))
        scv = lambda: sc_t[:, 0:nb].rearrange("p (b o) -> p b o", o=1).broadcast_to((128, nb, 32))
        P.op(eng, lambda e: e.tensor_tensor(xs_f().rearrange("p (b e) -> p b e", e=32),
                                            src_f().rearrange("p (b e) -> p b e", e=32),
                                            rsv(), AluOpType.mult),
             reads=extra_reads + [pref + "_rsp"], writes=[pref + "_xs"])
        P.op(eng, lambda e: e.tensor_scalar(xs_f(), xs_f(), 6.0, -6.0,
                                            AluOpType.min, AluOpType.max),
             reads=[pref + "_xs"], writes=[pref + "_xs"])
        P.op(eng, lambda e: e.tensor_scalar(mg_f().bitcast(I32), xs_f().bitcast(I32),
                                            EXPMASK, None, AluOpType.bitwise_and),
             reads=[pref + "_xs"], writes=[pref + "_mg"])
        P.op(eng, lambda e: e.tensor_scalar(mg_f(), mg_f(), 1.0, 6291456.0,
                                            AluOpType.max, AluOpType.mult),
             reads=[pref + "_mg"], writes=[pref + "_mg"])
        P.op(eng, lambda e: e.tensor_add(xs_f(), xs_f(), mg_f()),
             reads=[pref + "_xs", pref + "_mg"], writes=[pref + "_xs"])
        P.op(eng, lambda e: e.tensor_sub(xs_f(), xs_f(), mg_f()),
             reads=[pref + "_xs", pref + "_mg"], writes=[pref + "_xs"])
        P.op(eng, lambda e: e.tensor_tensor(dst_f().rearrange("p (b e) -> p b e", e=32),
                                            xs_f().rearrange("p (b e) -> p b e", e=32),
                                            scv(), AluOpType.mult),
             reads=[pref + "_xs", pref + "_scp"], writes=[pref + "_dst"] + list(dst_keys))

    if debug_taps:
        dbg_xn2 = nc.dram_tensor("dbg_xn2", [128, D], F32, kind="ExternalOutput")
    for i in range(NT):
        slot = i % 2
        P.op("vector", lambda e, i=i: e.tensor_scalar(xnt[:], xt[i][:],
                                                      negm[:, i:i + 1],
                                                      r_own[:, i:i + 1],
                                                      AluOpType.add, AluOpType.mult),
             reads=[("xt", i), "negm", "r_own", "q_dst"], writes=["xn", "qTr"])
        xreads = ["xn"]
        if debug_taps and i == NT - 1:
            P.op("sync", lambda e: e.dma_start(out=dbg_xn2[:, :], in_=xnt[:]),
                 reads=["xn"], writes=["xn_tap"], dma="dbg")
            xreads = ["xn", "xn_tap"]
        quant_chain("vector", lambda: xnt[:], lambda s=slot: xqt[s][:],
                    amax, qs_rs, qs_sc, lambda: xnt[:], lambda: mg_t[:],
                    D, "q", xreads, dst_keys=[("xqt", slot)])
        P.op("gpsimd", lambda e, i=i, slot=slot: e.dma_start(
            out=xq_dram[i * 128:(i + 1) * 128, :], in_=xqt[slot][:]),
            reads=["q_dst"], writes=[("xq_dram", i)], dma="xq")

    P.op("scalar", lambda e: e.activation(wfac[:], dwt[:], AF.Exp, scale=8.0),
         reads=["dwt"], writes=["wexp"])
    P.op("vector", lambda e: e.tensor_scalar(wfac[:], wfac[:], 1.0, None, AluOpType.add),
         reads=["wexp"], writes=["wexp"])
    P.op("vector", lambda e: e.reciprocal(wrfac[:], wfac[:]),
         reads=["wexp"], writes=["wrec"])
    P.op("vector", lambda e: e.tensor_scalar(dwt[:], wrfac[:], -2.0, 1.0,
                                             AluOpType.mult, AluOpType.add),
         reads=["wrec"], writes=["wtanh"])
    P.op("scalar", lambda e: e.activation(wfac[:], dwt[:], AF.Exp,
                                          scale=0.5 * LN2, bias=lnb_t[:, 0:1]),
         reads=["wtanh", "lnb0"], writes=["wfac4"])
    P.op("scalar", lambda e: e.activation(wrfac[:], dwt[:], AF.Exp,
                                          scale=-0.5 * LN2, bias=lnb_t[:, 1:2]),
         reads=["wtanh", "lnb1"], writes=["wrfac4"])
    for t in range(4):
        P.op("gpsimd", lambda e, t=t: e.dma_start(out=wt_in[:], in_=w_half[t * 128:(t + 1) * 128, :]),
             reads=["w_dst"], writes=["wt_in"] + WQ_ALIAS, dma="wl")
        P.op("gpsimd", lambda e, t=t: e.engine_nop(), reads=["wfac4", "wrfac4"], writes=["w_fac"])
        quant_chain("vector", lambda: wt_in[:], lambda: wt_in[:],
                    wamax, wrs, wsc, lambda: wt_in[:], lambda: wmg[:],
                    D, "w", ["wt_in"],
                    fac_mul=lambda t=t: wfac[:, t * NB:(t + 1) * NB],
                    fac_rs=lambda t=t: wrfac[:, t * NB:(t + 1) * NB])
        P.op("vector", lambda e: e.tensor_copy(wq_bf[:], wt_in[:]),
             reads=["w_dst"], writes=["wq_bf", "bias"])
        P.op("gpsimd", lambda e, sem, t=t: cond2(
            e, sem, lambda e2: e2.dma_start(out=wq_sh[t * 128:(t + 1) * 128, :], in_=wq_bf[:]),
            lambda e2: e2.dma_start(out=wq_sh[512 + t * 128:512 + (t + 1) * 128, :], in_=wq_bf[:])), reads=["wq_bf", "part_reg"], writes=[("wq_sh", t)], manual_sem=True)
    P.op("gpsimd", lambda e, sem: cond2(
        e, sem, lambda e2: e2.dma_start(out=wq_flag[0:1, :], in_=onesu[:]),
        lambda e2: e2.dma_start(out=wq_flag[1:2, :], in_=onesu[:])),
        reads=["onesu", "part_reg"] + [("wq_sh", t) for t in range(4)], writes=["wq_flag_w"],
        manual_sem=True)
    spin("wq", wq_flag[0:1, 0:1], wq_flag[1:2, 0:1])

    if stage <= 3:
        bailout()
        return nc

    if stage != 62:
        for d in range(8):
            P.op("sync", lambda e, d=d: e.dma_start(out=xqT_strip(d),
                                                    in_=xq_dram[:, d * 128:(d + 1) * 128],
                                                    transpose=True),
                 reads=[("xq_dram", i) for i in range(NT)] + [("pbuf_pv", b) for b in range(3)],
                 writes=XQT_KEYS, dma="xqT")
    if stage != 61:
        P.op("sync", lambda e: e.dma_start(out=wq_loc[:, :], in_=wq_sh[:, :]),
             reads=[("bar", "wq")], writes=["wq_loc"], dma="xqT")
        for d in range(8):
            P.op("sync", lambda e, d=d: e.dma_start(out=wqT_strip(d),
                                                    in_=wq_loc[:, d * 128:(d + 1) * 128],
                                                    transpose=True),
                 reads=["wq_loc", ("pbuf_pv", 2)],
                 writes=WQT_KEYS, dma="xqT")

    if stage in (6, 61, 62):
        bailout()
        return nc
    y_marks = []
    for u in range(NT if stage != 7 else 1):
        for oc in range(2):
            pslot = oc % 2
            ps_ = psA if pslot == 0 else psB
            pkey = PSA if pslot == 0 else PSB

            def ffn(e, u=u, oc=oc, ps_=ps_):
                last = None
                for d in range(8):
                    e.ldweights(xqT_strip(d)[:, u * 128:(u + 1) * 128])
                    last = e.matmul(ps_[:, 0:512], xqT_strip(d)[:, u * 128:(u + 1) * 128],
                                    wqT_strip(d)[:, oc * 512:(oc + 1) * 512],
                                    start=(d == 0), stop=False)
                last = e.matmul(ps_[:, 0:512], ones_row[0:1, :],
                                bias_bf[0:1, oc * 512:(oc + 1) * 512], start=False, stop=True)
                return last
            P.op("tensor", ffn, reads=XQT_KEYS + WQT_KEYS + ["ones_row", "bias_bf"],
                 writes=[pkey(0)])
            gch = lambda oc=oc: gt[:, oc * 512:(oc + 1) * 512]
            P.op("scalar", lambda e, oc=oc, ps_=ps_: e.activation(
                gch(oc), ps_[:, 0:512], AF.Exp, scale=-1.0),
                reads=[pkey(0), "gt_used"], writes=[("gt_e", oc)])
            P.op("scalar", lambda e, oc=oc: e.activation(
                gch(oc), gch(oc), AF.Ln, bias=onep[:]),
                reads=[("gt_e", oc), "onep"], writes=[("gt_l", oc)])
            P.op("scalar", lambda e, oc=oc: e.activation(
                gch(oc), gch(oc), AF.Exp, scale=-1.0),
                reads=[("gt_l", oc)], writes=[("gt_s", oc)])
            P.op("vector", lambda e, oc=oc, ps_=ps_: e.tensor_tensor(
                gch(oc), gch(oc), ps_[:, 0:512], AluOpType.mult),
                reads=[("gt_s", oc), pkey(0)], writes=[("gt", oc), ("xTs", 0)])

        def gate(e, u=u):
            sview = stok[:, u * H:(u + 1) * H].rearrange("p (o h) -> p o h", o=1)\
                .broadcast_to((128, 16, H))
            return e.scalar_tensor_tensor(ytt[:].rearrange("p (o h) -> p o h", o=16),
                                          gt[:].rearrange("p (o h) -> p o h", o=16),
                                          1.0, sview, AluOpType.mult, AluOpType.mult)
        P.op("vector", gate, reads=[("gt", 0), ("gt", 1), "stok"],
             writes=["ytt", "gt_used", ("xTs", 1)])
        P.op("vector", lambda e, u=u: e.tensor_add(ytt[:], ytt[:], xt[u][:]),
             reads=["ytt", ("xt", u)], writes=["ytt"])
        ym = P.op("sync", lambda e, u=u: e.dma_start(out=y_out[u * 128:(u + 1) * 128, :], in_=ytt[:]),
                  reads=["ytt"], writes=[("y_out", u)], dma="yo")
        y_marks.append(ym)

    if debug_taps:
        dbg_kv = nc.dram_tensor("dbg_kv", [256, 2 * R], BF16, kind="ExternalOutput")
        dbg_z = nc.dram_tensor("dbg_z", [2 * NG * 128, KPG], F32, kind="ExternalOutput")
        dbg_sf = nc.dram_tensor("dbg_sf", [64, R], F32, kind="ExternalOutput")
        dbg_st = nc.dram_tensor("dbg_st", [128, NT * H], F32, kind="ExternalOutput")
        dbg_zm = nc.dram_tensor("dbg_zm", [128, KT], F32, kind="ExternalOutput")
        dbg_q = nc.dram_tensor("dbg_q", [128, R], BF16, kind="ExternalOutput")
        dbg_kd = nc.dram_tensor("dbg_kd", [128, S], BF16, kind="ExternalOutput")
        dbg_vt = nc.dram_tensor("dbg_vt", [128, KT * H], BF16, kind="ExternalOutput")
        dbg_xq = nc.dram_tensor("dbg_xq", [R, D], BF16, kind="ExternalOutput")
        dbg_wq = nc.dram_tensor("dbg_wq", [D, D], BF16, kind="ExternalOutput")
        dbg_g = nc.dram_tensor("dbg_g", [128, D], F32, kind="ExternalOutput")
        ym = P.op("sync", lambda e: e.dma_start(out=dbg_kv[:, :], in_=kv_sh[:, :]),
                  reads=["kv_sh_w", ("bar", "kv")], dma="dbg")
        y_marks.append(ym)
        ym = P.op("sync", lambda e: e.dma_start(
            out=dbg_z[:, :].rearrange("(a p) k -> a p k", p=128),
            in_=z_sh[:, :, :]), reads=[("z_sh", g) for g in range(NG)], dma="dbg")
        y_marks.append(ym)
        ym = P.op("sync", lambda e: e.dma_start(out=dbg_sf[:, :], in_=sfeat[:, :]),
                  reads=["sfeat"], dma="dbg")
        y_marks.append(ym)
        ym = P.op("sync", lambda e: e.dma_start(out=dbg_st[:, :], in_=stok[:, :]),
                  reads=["stok"], dma="dbg")
        y_marks.append(ym)
        ym = P.op("sync", lambda e: e.dma_start(out=dbg_zm[:, :], in_=zmine[:, :]),
                  reads=[("zmine", g) for g in range(NG)], dma="dbg")
        y_marks.append(ym)
        ym = P.op("sync", lambda e: e.dma_start(out=dbg_q[:, :], in_=qTo[:, :]),
                  reads=["qTo"], dma="dbg")
        y_marks.append(ym)
        ym = P.op("sync", lambda e: e.dma_start(out=dbg_kd[:, :], in_=kTd[:, :]),
                  reads=["kTd_m", "kTd_p"], dma="dbg")
        y_marks.append(ym)
        ym = P.op("sync", lambda e: e.dma_start(out=dbg_vt[:, :], in_=vtok[:, :]),
                  reads=["vtok"], dma="dbg")
        y_marks.append(ym)
        ym = P.op("sync", lambda e: e.dma_start(out=dbg_xq[:, :], in_=xq_dram[:, :]),
                  reads=[("xq_dram", i) for i in range(NT)], dma="dbg")
        y_marks.append(ym)
        ym = P.op("sync", lambda e: e.dma_start(out=dbg_wq[:, :], in_=wq_sh[:, :]),
                  reads=[("bar", "wq")], dma="dbg")
        y_marks.append(ym)
        ym = P.op("sync", lambda e: e.dma_start(out=dbg_g[:, :], in_=gt[:, :]),
                  reads=[("gt", 0), ("gt", 1)], dma="dbg")
        y_marks.append(ym)
        dbg_cv = nc.dram_tensor("dbg_cv", [128, 8], F32, kind="ExternalOutput")
        dbg_qs = nc.dram_tensor("dbg_qs", [128, 3 * NB], F32, kind="ExternalOutput")
        dbg_xn = nc.dram_tensor("dbg_xn", [128, D], F32, kind="ExternalOutput")
        ym = P.op("sync", lambda e: e.dma_start(out=dbg_cv[:, 0:4], in_=cact[:, :]),
                  reads=["cact1"], dma="dbg")
        y_marks.append(ym)
        ym = P.op("sync", lambda e: e.dma_start(out=dbg_cv[:, 4:6], in_=cvec[:, :]),
                  reads=["cvec"], dma="dbg")
        y_marks.append(ym)
        ym = P.op("sync", lambda e: e.dma_start(out=dbg_cv[:, 6:8], in_=lnb_t[:, :]),
                  reads=["lnb0", "lnb1"], dma="dbg")
        y_marks.append(ym)
        ym = P.op("sync", lambda e: e.dma_start(out=dbg_qs[:, 0:NB], in_=amax[:, :]),
                  reads=["q_amax"], dma="dbg")
        y_marks.append(ym)
        ym = P.op("sync", lambda e: e.dma_start(out=dbg_qs[:, NB:2 * NB], in_=qs_rs[:, :]),
                  reads=["q_rsp"], dma="dbg")
        y_marks.append(ym)
        ym = P.op("sync", lambda e: e.dma_start(out=dbg_qs[:, 2 * NB:3 * NB], in_=qs_sc[:, :]),
                  reads=["q_scp"], dma="dbg")
        y_marks.append(ym)
        ym = P.op("sync", lambda e: e.dma_start(out=dbg_xn[:, :], in_=xnt[:]),
                  reads=["xn"], dma="dbg")
        y_marks.append(ym)
        dbg_mr = nc.dram_tensor("dbg_mr", [128, 4 * NT], F32, kind="ExternalOutput")
        ym = P.op("sync", lambda e: e.dma_start(out=dbg_mr[:, 0:NT], in_=m_own[:, :]),
                  reads=["m_own"], dma="dbg")
        y_marks.append(ym)
        ym = P.op("sync", lambda e: e.dma_start(out=dbg_mr[:, NT:2 * NT], in_=r_own[:, :]),
                  reads=["r_own"], dma="dbg")
        y_marks.append(ym)
        ym = P.op("sync", lambda e: e.dma_start(out=dbg_mr[:, 2 * NT:3 * NT], in_=negmr[:, :]),
                  reads=["negmr"], dma="dbg")
        y_marks.append(ym)
        ym = P.op("sync", lambda e: e.dma_start(out=dbg_mr[:, 3 * NT:4 * NT], in_=var_t[:, :]),
                  reads=["var"], dma="dbg")
        y_marks.append(ym)

    def final_wait(e):
        for sname, val in y_marks:
            e.wait_ge(P.sems[sname], val)
        return None
    P.raw("sync", final_wait)
    P.emit()
    return nc


def kernel(L0_attn_x, L0_attn_Wq, L0_attn_Wk, L0_attn_Wv,
           L0_ffn_weight, L0_ffn_bias, L0_ffn_delta_w, L0_ffn_delta_act):
    x = np.ascontiguousarray(np.asarray(L0_attn_x, dtype=np.float32))
    B, S, _ = x.shape
    R = S // 4
    nc = build_nc(R=R)
    ident = np.eye(128, dtype=bf16)
    wq = np.ascontiguousarray(np.asarray(L0_attn_Wq, dtype=np.float32))
    wk = np.ascontiguousarray(np.asarray(L0_attn_Wk, dtype=np.float32))
    wv = np.ascontiguousarray(np.asarray(L0_attn_Wv, dtype=np.float32))
    w = np.asarray(L0_ffn_weight, dtype=np.float32)
    dw = np.asarray(L0_ffn_delta_w, dtype=np.float32)
    bias = np.ascontiguousarray(np.asarray(L0_ffn_bias, dtype=np.float32)[None, :])
    dact = np.ascontiguousarray(
        np.broadcast_to(np.asarray(L0_ffn_delta_act, dtype=np.float32).reshape(1, 1), (128, 1)))
    in_maps = []
    for c in range(8):
        b = c // 4
        qr = c % 4
        q2 = qr ^ 2
        par = c % 2
        xb = x[b]
        xTb = np.ascontiguousarray(xb.T)
        xT_half = np.ascontiguousarray(
            np.concatenate([xTb[:, qr * R:(qr + 1) * R], xTb[:, q2 * R:(q2 + 1) * R]], axis=1))
        in_maps.append({
            "x_own": np.ascontiguousarray(xb[qr * R:(qr + 1) * R]),
            "x_red": np.ascontiguousarray(xb[q2 * R:(q2 + 1) * R]),
            "xT_half": xT_half,
            "wq_in": wq, "wk_in": wk, "wv_in": wv,
            "w_half": np.ascontiguousarray(w[par * 512:(par + 1) * 512]),
            "dw_half": np.ascontiguousarray(dw[par * 512:(par + 1) * 512]),
            "bias_in": bias, "dact_in": dact,
            "ident_bf": ident,
            "par_in": np.array([[par]], dtype=np.uint32),
        })
    res = run_bass_kernel_spmd(nc, in_maps, core_ids=list(range(8)))
    y = np.empty((B, S, D), dtype=np.float32)
    for c in range(8):
        b = c // 4
        qr = c % 4
        y[b, qr * R:(qr + 1) * R] = res.results[c]["y_out"]
    return y
